# version 08 (final)
# baseline (speedup 1.0000x reference)
import numpy as np
import ml_dtypes
import concourse.bass as bass
import concourse.mybir as mybir
from concourse.bass_utils import run_bass_kernel_spmd
from contextlib import ExitStack

F32 = mybir.dt.float32
BF16 = mybir.dt.bfloat16
AF = mybir.ActivationFunctionType
ALU = mybir.AluOpType
AX = mybir.AxisListType

T = 2048
D = 1024
NH = 16
HD = 64
INNER = 4096
NT = T // 128
NPAIR = NH // 2
MAGIC = float(np.float32(1.5 * 2.0 ** 23))
EPS = 1e-5
ENGS = ("sync", "gpsimd", "vector", "scalar", "tensor")
ATTN_A_PARTS = "full"
DBG_OBUF = False
DBG_FFN = False


def _gammas():
    lin = np.linspace(np.log(1.0 / 512.0), np.log(1.0 / 32.0), NH)
    return 1.0 - np.exp(lin)


def host_constants():
    g = _gammas()
    c = {}
    c["ident"] = np.eye(128).astype(ml_dtypes.bfloat16)
    c["triT"] = (np.arange(128)[:, None] <= np.arange(128)[None, :]).astype(ml_dtypes.bfloat16)
    qs = np.zeros((NPAIR, 128, 128), np.float64)
    kn = np.zeros((NPAIR, 128, 128), np.float64)
    kp = np.zeros((NPAIR, 128, 128), np.float64)
    s_ = np.arange(128, dtype=np.float64)
    for pr in range(NPAIR):
        for half in range(2):
            gh = g[2 * pr + half]
            qs[pr, 64 * half:64 * half + 64, :] = gh ** s_[None, :]
            kn[pr, 64 * half:64 * half + 64, :] = gh ** (-s_[None, :])
            kp[pr, :, 64 * half:64 * half + 64] = (gh ** (128.0 - s_))[:, None]
    c["qs"] = qs.astype(ml_dtypes.bfloat16)
    c["kn"] = kn.astype(ml_dtypes.bfloat16)
    c["kp"] = kp.astype(ml_dtypes.bfloat16)
    inv = 1.0 / 10000.0 ** (np.arange(0, HD, 2, dtype=np.float64) / HD)
    fr = np.arange(T, dtype=np.float64)[:, None] * inv[None, :]
    cos, sin = np.cos(fr).T, np.sin(fr).T
    C = np.zeros((128, T), np.float64)
    Sn = np.zeros((128, T), np.float64)
    for half in range(2):
        for j in range(32):
            C[64 * half + 2 * j] = cos[j]; C[64 * half + 2 * j + 1] = cos[j]
            Sn[64 * half + 2 * j] = sin[j]; Sn[64 * half + 2 * j + 1] = sin[j]
    c["ropeC"] = C.astype(np.float32)
    c["ropeS"] = Sn.astype(np.float32)
    return c


class Prog:
    def __init__(self, nc):
        self.nc = nc
        self.prog = []
        self.cnt = {e: 0 for e in ENGS}
        self.dma_cnt = {"sync": 0, "gpsimd": 0}
        self.sems = {}
        self.last_wait = {}

    def op(self, eng, fn):
        assert eng in ("vector", "scalar", "gpsimd")
        n = self.cnt[eng]
        sem = self.sems["c_" + eng]

        def thunk(e, fn=fn, n=n, sem=sem):
            if n > 0:
                e.wait_ge(sem, n)
            fn(e).then_inc(sem, 1)
        self.prog.append((eng, thunk))
        self.cnt[eng] = n + 1
        return (eng, n + 1)

    def pe(self, fn, inc=False):
        if inc:
            sem = self.sems["c_tensor"]

            def thunk(e, fn=fn, sem=sem):
                fn(e).then_inc(sem, 1)
            self.prog.append(("tensor", thunk))
            self.cnt["tensor"] += 1
            return ("tensor", self.cnt["tensor"])
        self.prog.append(("tensor", lambda e, fn=fn: fn(e)))
        return None

    def dma(self, queue, out, in_, **kw):
        self.dma_cnt[queue] += 16
        sem = self.sems["d_" + queue]

        def thunk(e, out=out, in_=in_, kw=kw, sem=sem):
            e.dma_start(out=out, in_=in_, **kw).then_inc(sem, 16)
        self.prog.append((queue, thunk))
        return ("d" + queue, self.dma_cnt[queue])

    def wait(self, eng, token):
        if token is None:
            return
        kind, val = token
        sem_name = ("d_" + kind[1:]) if kind.startswith("d") else ("c_" + kind)
        sem = self.sems[sem_name]
        key = (eng, sem_name)
        if self.last_wait.get(key, 0) >= val:
            return
        self.last_wait[key] = val

        def thunk(e, sem=sem, val=val):
            e.wait_ge(sem, val)
        self.prog.append((eng, thunk))

    def now(self, eng):
        return (eng, self.cnt[eng]) if self.cnt[eng] else None

    def dnow(self, queue):
        return ("d" + queue, self.dma_cnt[queue]) if self.dma_cnt[queue] else None

    def barrier(self):
        toks = [t for t in (self.now("vector"), self.now("scalar"),
                            self.now("gpsimd"), self.now("tensor"),
                            self.dnow("sync"), self.dnow("gpsimd")) if t]
        for e in ENGS:
            for t in toks:
                self.wait(e, t)

    def realize(self):
        prog = self.prog
        with self.nc.Block() as block:
            def runner(name):
                def body(e):
                    for eng, fn in prog:
                        if eng == name:
                            fn(e)
                return body
            block.sync(runner("sync"))
            block.gpsimd(runner("gpsimd"))
            block.vector(runner("vector"))
            block.scalar(runner("scalar"))
            block.tensor(runner("tensor"))


def build(nstages=4, debug_taps=()):
    nc = bass.Bass("TRN2", target_bir_lowering=False, debug=False)
    p = Prog(nc)
    ctx = ExitStack()

    DT = lambda n, sh, dt=F32, kind="ExternalInput": nc.dram_tensor(n, sh, dt, kind=kind).ap()
    x_in = DT("x_in", [T, D])
    x_out = DT("x_out", [T, D], F32, "ExternalOutput")
    dbg = {i: DT(f"dbg{i}", [T, D], F32, "ExternalOutput") for i in debug_taps}
    wq_a = DT("wq_a", [NH, HD, HD]); wk_a = DT("wk_a", [NH, HD, HD]); wv_a = DT("wv_a", [NH, HD, HD])
    w1_a = DT("w1_a", [D, D]); w2_a = DT("w2_a", [D, D])
    gns = DT("gns", [D]); gnb = DT("gnb", [D])
    Wq_b = DT("Wq_b", [D, D]); Wk_b = DT("Wk_b", [D, D]); Wv_b = DT("Wv_b", [D, D]); Wo_b = DT("Wo_b", [D, D])
    wg_f = DT("wg_f", [INNER, D]); wu_f = DT("wu_f", [INNER, D]); wd_f = DT("wd_f", [D, INNER])
    ident_in = DT("identc", [128, 128], BF16)
    triT_in = DT("triTc", [128, 128], BF16)
    qs_in = DT("qsc", [NPAIR, 128, 128], BF16)
    kn_in = DT("knc", [NPAIR, 128, 128], BF16)
    kp_in = DT("kpc", [NPAIR, 128, 128], BF16)
    ropeC_in = DT("ropeCc", [128, T], F32)
    ropeS_in = DT("ropeSc", [128, T], F32)
    dbg_obuf = DT("dbg_obuf", [128, NT * D], BF16, "ExternalOutput") if DBG_OBUF else None
    dbg_fa = DT("dbg_fa", [128, 17408], F32, "ExternalOutput") if DBG_FFN else None
    dbg_fy = DT("dbg_fy", [128, 4096], BF16, "ExternalOutput") if DBG_FFN else None
    dbg_fb = DT("dbg_fb", [128, 2048], F32, "ExternalOutput") if DBG_FFN else None
    dbg_fr = DT("dbg_fr", [1, 1536], F32, "ExternalOutput") if DBG_FFN else None
    dbg_fm = DT("dbg_fm", [1, 8], F32, "ExternalOutput") if DBG_FFN else None
    tern_g = DT("tern_g", [8, 128, INNER], BF16, "Internal")
    tern_u = DT("tern_u", [8, 128, INNER], BF16, "Internal")
    tern_d = DT("tern_d", [32, 128, D], BF16, "Internal")
    v_scr = DT("v_scr", [T, NH * 65], BF16, "Internal")

    for s in ("c_vector", "c_scalar", "c_gpsimd", "c_tensor", "d_sync", "d_gpsimd"):
        p.sems[s] = ctx.enter_context(nc.semaphore(s))

    S_ = lambda n, sh, dt: ctx.enter_context(nc.sbuf_tensor(n, sh, dt))
    x_res = S_("x_res", [128, NT * D], F32)
    ident = S_("ident_t", [128, 128], BF16)
    triT = S_("triT_t", [128, 128], BF16)
    qs_t = S_("qs_t", [128, NPAIR * 128], BF16)
    kn_t = S_("kn_t", [128, NPAIR * 128], BF16)
    kp_t = S_("kp_t", [128, NPAIR * 128], BF16)
    ones_f = S_("ones_f", [128, 128], F32)
    ones_b = S_("ones_b", [128, 128], BF16)
    gs_row = S_("gs_row", [1, 1024], F32)
    gb_colb = S_("gb_colb", [128, 8], BF16)
    state_f = S_("state_f", [128, NPAIR * HD], F32)
    state_b = S_("state_b", [128, NPAIR * HD], BF16)
    cols = S_("colsbank", [128, 64], F32)
    magic_col = S_("magic_col", [128, 1], F32)
    eps_col = S_("eps_col", [128, 1], F32)
    mw_cols = S_("mw_cols", [128, 4], F32)
    PS = lambda n, sh, dt=F32: ctx.enter_context(nc.psum_tensor(n, sh, dt))
    pS4 = PS("pS4", [128, 2048])
    pA = PS("pA", [128, 512]); pB = PS("pB", [128, 512])
    pT0 = PS("pT0", [128, 1024], BF16)
    pT1 = PS("pT1", [128, 1024], BF16)
    pT = [pT0, pT1]
    pS = [pS4[:, i * 512:(i + 1) * 512] for i in range(4)]

    tok_x = p.dma("sync", x_res[:].rearrange("q (tt d) -> q tt d", d=D),
                  x_in.rearrange("(tt q) d -> q tt d", q=128))
    for dst, src in ((ident, ident_in), (triT, triT_in)):
        p.dma("sync", dst[:], src)
    for pr in range(NPAIR):
        p.dma("sync", qs_t[:, pr * 128:pr * 128 + 128], qs_in[pr])
        p.dma("sync", kn_t[:, pr * 128:pr * 128 + 128], kn_in[pr])
        p.dma("sync", kp_t[:, pr * 128:pr * 128 + 128], kp_in[pr])
    p.dma("sync", gs_row[0:1, :], gns.rearrange("(o d) -> o d", o=1))
    p.dma("gpsimd", gb_colb[:], gnb.rearrange("(c q) -> q c", q=128), allow_slow_non_contiguous=True)
    tok_c = p.dnow("sync")
    p.wait("vector", tok_c)
    p.wait("vector", p.dnow("gpsimd"))
    p.op("vector", lambda v: v.memset(magic_col[:], MAGIC))
    p.op("vector", lambda v: v.memset(eps_col[:], EPS))
    p.op("vector", lambda v: v.memset(cols[:, 63:64], 0.0))
    p.op("vector", lambda v: v.tensor_add(out=cols[:, 63:64], in0=cols[:, 63:64], in1=magic_col[:]))
    p.op("vector", lambda v: v.memset(ones_f[:], 1.0))
    p.op("vector", lambda v: v.memset(ones_b[:], 1.0))
    p.op("vector", lambda v: v.memset(state_f[:], 0.0))
    tok_init = p.op("vector", lambda v: v.memset(state_b[:], 0.0))

    colptr = [0]

    def col(n=1):
        i = colptr[0]
        colptr[0] = (i + n) % 56
        if colptr[0] < i:
            i, colptr[0] = 0, n
        return cols[:, i:i + n] if n > 1 else cols[:, i:i + 1]

    tp_state = {"last": [None, None], "j": 0, "lastA": None}

    def pA_done(tok):
        tp_state["lastA"] = tok

    def pe_transpose(src, dst, tok_src_v=None, f32=False, evac=True):
        pr_, cr_ = src.shape[0], src.shape[1]
        if f32:
            bank = pA[0:cr_, 0:pr_]
            if tp_state["lastA"] is not None:
                p.wait("tensor", tp_state["lastA"])
            idn = ident_f
        else:
            j = tp_state["j"] % 2
            tp_state["j"] += 1
            bank = pT[j][0:cr_, 0:pr_]
            if tp_state["last"][j] is not None:
                p.wait("tensor", tp_state["last"][j])
            idn = ident
        if tok_src_v is not None:
            p.wait("tensor", tok_src_v)
        tpe = p.pe(lambda t, src=src, bank=bank, idn=idn: t.transpose(
            out=bank, in_=src, identity=idn[:]), inc=True)
        if not evac:
            return tpe
        p.wait("vector", tpe)
        tok = p.op("vector", lambda v, dst=dst, bank=bank: v.tensor_copy(out=dst, in_=bank))
        if f32:
            tp_state["lastA"] = tok
        else:
            tp_state["last"][j] = tok
        return tok

    ident_f = S_("ident_f", [128, 128], F32)
    p.wait("vector", tok_c)
    p.op("vector", lambda v: v.tensor_copy(out=ident_f[:], in_=ident[:]))

    def x_tile(tt):
        return x_res[:, tt * D:(tt + 1) * D]

    def ln_stats(xt, want_quant, scr1024):
        c_sumsq = col(); c_sum = col(); c_mean = col(); c_var = col()
        c_rstd = col(); c_tmp = col()
        tok_sq = p.op("scalar", lambda s: s.activation(out=scr1024, in_=xt, func=AF.Square, accum_out=c_sumsq))
        tok_sq = p.op("scalar", lambda s: s.activation(out=c_tmp, in_=c_sumsq, func=AF.Copy))
        p.op("vector", lambda v: v.reduce_sum(out=c_sum, in_=xt, axis=AX.X))
        res = {}
        if want_quant:
            c_max = col(); c_negmin = col(); c_amax = col(); c_s = col(); c_rcps = col()
            p.op("vector", lambda v: v.reduce_max(out=c_max, in_=xt, axis=AX.X))
            p.wait("vector", tok_sq)
            p.op("vector", lambda v: v.tensor_scalar_mul(out=scr1024, in0=xt, scalar1=-1.0))
            p.op("vector", lambda v: v.reduce_max(out=c_negmin, in_=scr1024, axis=AX.X))
        p.wait("vector", tok_sq)
        p.op("vector", lambda v: v.tensor_scalar_mul(out=c_mean, in0=c_sum, scalar1=1.0 / D))
        p.op("vector", lambda v: v.tensor_mul(out=c_tmp, in0=c_mean, in1=c_mean))
        p.op("vector", lambda v: v.tensor_scalar_mul(out=c_var, in0=c_sumsq, scalar1=1.0 / D))
        p.op("vector", lambda v: v.tensor_sub(out=c_var, in0=c_var, in1=c_tmp))
        p.op("vector", lambda v: v.tensor_scalar_add(out=c_var, in0=c_var, scalar1=EPS))
        tv = p.op("vector", lambda v: v.tensor_copy(out=c_tmp, in_=c_var))
        p.wait("scalar", tv)
        p.op("scalar", lambda s: s.activation(out=c_rstd, in_=c_var, func=AF.Sqrt))
        ts_ = p.op("scalar", lambda s: s.activation(out=c_tmp, in_=c_rstd, func=AF.Copy))
        p.wait("vector", ts_)
        p.op("vector", lambda v: v.reciprocal(out=c_rstd, in_=c_rstd))
        c_scale = col(); c_bias = col()
        if not want_quant:
            p.op("vector", lambda v: v.tensor_copy(out=c_scale, in_=c_rstd))
        else:
            p.op("vector", lambda v: v.tensor_sub(out=c_tmp, in0=c_max, in1=c_mean))
            p.op("vector", lambda v: v.tensor_add(out=c_amax, in0=c_mean, in1=c_negmin))
            p.op("vector", lambda v: v.tensor_max(out=c_amax, in0=c_amax, in1=c_tmp))
            p.op("vector", lambda v: v.tensor_mul(out=c_amax, in0=c_amax, in1=c_rstd))
            p.op("vector", lambda v: v.tensor_scalar_max(out=c_amax, in0=c_amax, scalar1=1e-5))
            p.op("vector", lambda v: v.reciprocal(out=c_s, in_=c_amax))
            p.op("vector", lambda v: v.tensor_scalar_mul(out=c_s, in0=c_s, scalar1=127.0))
            p.op("vector", lambda v: v.tensor_scalar_mul(out=c_rcps, in0=c_amax, scalar1=1.0 / 127.0))
            p.op("vector", lambda v: v.tensor_mul(out=c_scale, in0=c_rstd, in1=c_s))
            res["rcp_s"] = c_rcps
        p.op("vector", lambda v: v.tensor_mul(out=c_bias, in0=c_mean, in1=c_scale))
        p.op("vector", lambda v: v.tensor_scalar_mul(out=c_bias, in0=c_bias, scalar1=-1.0))
        tok = p.op("vector", lambda v: v.tensor_add(out=c_tmp, in0=c_scale, in1=c_bias))
        if want_quant:
            tok = p.op("vector", lambda v: v.tensor_add(out=c_tmp, in0=c_tmp, in1=res["rcp_s"]))
        res.update(scale=c_scale, bias=c_bias, tok=tok)
        return res

    def stage_attn_a():
        _attn_a()

    def stage_ffn(first):
        _ffn(first)

    def stage_attn_b():
        _attn_b()

    def _attn_a():
        g = _gammas()
        sctx = ExitStack()
        A = lambda n, sh, dt: sctx.enter_context(nc.sbuf_tensor(n, sh, dt))
        y1fm = A("y1fm", [128, 8 * T], BF16)
        obuf = A("obuf", [128, NT * D], BF16)
        w1s = A("w1s", [128, 8 * D], BF16)
        w2s = A("w2s", [128, 8 * D], BF16)
        wbd = A("wbd", [128, 3 * NPAIR * 128], BF16)
        gbw2_b = A("gbw2_b", [128, D], F32)
        gsb = A("gsb", [128, D], F32)
        scr = A("scr_a", [128, 1536], F32)
        sb = A("sb_a", [128, 1792], BF16)
        w1_c = w1_a.rearrange("(c q) o -> c q o", q=128)
        w2_c = w2_a.rearrange("(c q) o -> c q o", q=128)
        for c in range(8):
            p.dma("gpsimd", w1s[:, c * D:(c + 1) * D], w1_c[c])
            p.dma("gpsimd", w2s[:, c * D:(c + 1) * D], w2_c[c])
        tok_w = p.dnow("gpsimd")
        p.wait("tensor", tok_w)
        tokmm = None
        for o in range(2):
            for c in range(8):
                tokmm = p.pe(lambda t, c=c, o=o: t.matmul(
                    out=pS[o][0:1, :], lhsT=gb_colb[:, c:c + 1],
                    rhs=w2s[:, c * D + o * 512:c * D + o * 512 + 512],
                    start=(c == 0), stop=(c == 7)), inc=(c == 7))
        p.wait("vector", tokmm)
        p.op("vector", lambda v: v.tensor_copy(out=scr[0:1, 0:512], in_=pS[0][0:1, :]))
        p.op("vector", lambda v: v.tensor_copy(out=scr[0:1, 512:1024], in_=pS[1][0:1, :]))
        tok = p.op("vector", lambda v: v.tensor_copy(out=sb[0:1, 0:1024], in_=scr[0:1, 0:1024]))
        p.wait("tensor", tok)
        for o in range(2):
            tokmm = p.pe(lambda t, o=o: t.matmul(
                out=pS[2 + o][:], lhsT=ones_b[0:1, :], rhs=sb[0:1, o * 512:o * 512 + 512],
                start=True, stop=True), inc=True)
        for o in range(2):
            tokg = p.pe(lambda t, o=o: t.matmul(
                out=pS[o][:], lhsT=ones_f[0:1, :], rhs=gs_row[0:1, o * 512:o * 512 + 512],
                start=True, stop=True), inc=True)
        p.wait("vector", tokg)
        p.op("vector", lambda v: v.tensor_copy(out=gbw2_b[:, 0:512], in_=pS[2][:]))
        p.op("vector", lambda v: v.tensor_copy(out=gbw2_b[:, 512:1024], in_=pS[3][:]))
        p.op("vector", lambda v: v.tensor_copy(out=gsb[:, 0:512], in_=pS[0][:]))
        tok_gb = p.op("vector", lambda v: v.tensor_copy(out=gsb[:, 512:1024], in_=pS[1][:]))
        tok_ms = p.op("vector", lambda v: v.memset(wbd[:], 0.0))
        p.wait("gpsimd", tok_ms)
        for wi, wsrc in enumerate((wq_a, wk_a, wv_a)):
            for h in range(NH):
                pr, half = h // 2, h % 2
                dst = wbd[64 * half:64 * half + 64,
                          wi * 1024 + pr * 128 + 64 * half: wi * 1024 + pr * 128 + 64 * half + 64]
                p.dma("gpsimd", dst, wsrc[h])
        tok_bd = p.dnow("gpsimd")
        if ATTN_A_PARTS == "prep":
            p.barrier(); sctx.close(); return

        p.wait("tensor", tok_gb)
        for tt in range(NT):
            colptr[0] = 0
            st = ln_stats(x_tile(tt), False, scr[:, 0:1024])
            p.wait("scalar", st["tok"])
            ty = p.op("scalar", lambda s, tt=tt, st=st: s.activation(
                out=sb[:, 0:1024], in_=x_tile(tt), func=AF.Identity,
                scale=st["scale"], bias=st["bias"]))
            for i in range(8):
                pe_transpose(sb[:, i * 128:(i + 1) * 128],
                             y1fm[:, i * T + tt * 128: i * T + tt * 128 + 128],
                             tok_src_v=ty)
            p.wait("scalar", p.now("tensor"))
        tok_y1 = p.now("vector")
        if ATTN_A_PARTS == "ln":
            p.barrier(); sctx.close(); return

        p.wait("tensor", tok_y1)
        evac_tok = [None, None]
        for tt in range(NT):
            for o in range(2):
                if evac_tok[o] is not None:
                    p.wait("tensor", evac_tok[o])
                for i in range(8):
                    mt = p.pe(lambda t, tt=tt, o=o, i=i: t.matmul(
                        out=pS[o][:],
                        lhsT=y1fm[:, i * T + tt * 128: i * T + tt * 128 + 128],
                        rhs=w1s[:, i * D + o * 512: i * D + o * 512 + 512],
                        start=(i == 0), stop=(i == 7)), inc=(i == 7))
                p.wait("scalar", mt)
                evac_tok[o] = p.op("scalar", lambda s, tt=tt, o=o: s.activation(
                    out=obuf[:, tt * D + o * 512: tt * D + o * 512 + 512],
                    in_=pS[o][:], func=AF.Silu))
        p.wait("tensor", evac_tok[0]); p.wait("tensor", evac_tok[1])
        p.wait("tensor", tok_bd)
        if ATTN_A_PARTS == "gate":
            p.barrier(); sctx.close(); return

        qsb = sb[:, 0:128]; knb = sb[:, 128:256]
        vtm = sb[:, 256:384]; kpb = sb[:, 384:512]
        smb = [sb[:, 512:640], sb[:, 640:768]]
        nb = scr[:, 0:1024]
        nbb = sb[:, 768:768 + 1024]
        gnst = scr[:, 1024:1024 + 24]
        banktok = {}
        state_cp = {}
        nblocks = 1 if ATTN_A_PARTS.startswith("blocks1") else NT
        for b in range(nblocks):
            for pr in range(NPAIR):
                ysl = y1fm[:, pr * T + b * 128: pr * T + b * 128 + 128]
                if banktok.get(0) is not None:
                    p.wait("tensor", banktok[0])
                p.pe(lambda t, pr=pr, ysl=ysl: t.matmul(
                    out=pS4[:, 0:128], lhsT=wbd[:, 0 * 1024 + pr * 128: 0 * 1024 + pr * 128 + 128],
                    rhs=ysl, start=True, stop=True))
                tokp2 = p.pe(lambda t, pr=pr, ysl=ysl: t.matmul(
                    out=pS4[:, 128:256], lhsT=wbd[:, 1 * 1024 + pr * 128: 1 * 1024 + pr * 128 + 128],
                    rhs=ysl, start=True, stop=True), inc=True)
                p.pe(lambda t, pr=pr, ysl=ysl: t.matmul(
                    out=pS4[:, 512:640], lhsT=ysl,
                    rhs=wbd[:, 2 * 1024 + pr * 128: 2 * 1024 + pr * 128 + 128],
                    start=True, stop=True))
                tokp = p.pe(lambda t, pr=pr, ysl=ysl: t.matmul(
                    out=pS4[:, 640:768], lhsT=ysl,
                    rhs=wbd[:, 1 * 1024 + pr * 128: 1 * 1024 + pr * 128 + 128],
                    start=True, stop=True), inc=True)
                p.wait("vector", tokp2)
                p.op("vector", lambda v, pr=pr: v.tensor_mul(
                    out=qsb, in0=pS4[:, 0:128], in1=qs_t[:, pr * 128:pr * 128 + 128]))
                tokkn = p.op("vector", lambda v, pr=pr: v.tensor_mul(
                    out=knb, in0=pS4[:, 128:256], in1=kn_t[:, pr * 128:pr * 128 + 128]))
                p.wait("vector", tokp)
                p.op("vector", lambda v: v.tensor_copy(out=vtm, in_=pS4[:, 512:640]))
                tokev = p.op("vector", lambda v, pr=pr: v.tensor_mul(
                    out=kpb, in0=pS4[:, 640:768], in1=kp_t[:, pr * 128:pr * 128 + 128]))
                banktok[0] = tokev
                if ATTN_A_PARTS == "blocks1a":
                    continue
                if banktok.get(2) is not None:
                    p.wait("tensor", banktok[2])
                p.wait("tensor", tokkn)
                for half in range(2):
                    p.pe(lambda t, half=half: t.matmul(
                        out=pS4[:, 1024 + 512 * half: 1024 + 512 * half + 128],
                        lhsT=knb[64 * half:64 * half + 64, :],
                        rhs=qsb[64 * half:64 * half + 64, :], start=True, stop=True),
                        inc=(half == 1))
                toks = p.now("tensor")
                p.wait("vector", toks)
                tokm = None
                for half in range(2):
                    tokm = p.op("vector", lambda v, half=half: v.tensor_mul(
                        out=smb[half], in0=pS4[:, 1024 + 512 * half:1024 + 512 * half + 128],
                        in1=triT[:]))
                banktok[2] = tokm
                if ATTN_A_PARTS == "blocks1b":
                    continue
                if banktok.get(3) is not None:
                    p.wait("tensor", banktok[3])
                p.wait("tensor", tokm)
                if state_cp.get(pr) is not None:
                    p.wait("tensor", state_cp[pr])
                p.wait("tensor", tokev)
                rslc = [slice(256, 320), slice(768, 832)]
                for half in range(2):
                    p.pe(lambda t, half=half: t.matmul(
                        out=pS4[:, rslc[half]],
                        lhsT=smb[half], rhs=vtm[:, 64 * half:64 * half + 64],
                        start=True, stop=False))
                    p.pe(lambda t, half=half, pr=pr: t.matmul(
                        out=pS4[:, rslc[half]],
                        lhsT=qsb[64 * half:64 * half + 64, :],
                        rhs=state_b[64 * half:64 * half + 64, pr * 64:pr * 64 + 64],
                        start=False, stop=True), inc=(half == 1))
                tokr = p.now("tensor")
                stb = pB if (pr % 2 == 0) else pA
                if banktok.get(("st", pr % 2)) is not None:
                    p.wait("tensor", banktok[("st", pr % 2)])
                p.pe(lambda t, stb=stb: t.matmul(out=stb[:, 0:64], lhsT=kpb, rhs=vtm[:, 0:64],
                                        start=True, stop=True))
                tokst = p.pe(lambda t, stb=stb: t.matmul(out=stb[:, 64:128], lhsT=kpb, rhs=vtm[:, 64:128],
                                                start=True, stop=True), inc=True)
                p.wait("vector", tokst)
                for half in range(2):
                    gp = float(g[2 * pr + half] ** 128)
                    sl = state_f[64 * half:64 * half + 64, pr * 64:pr * 64 + 64]
                    psl = stb[64 * half:64 * half + 64, 64 * half:64 * half + 64]
                    p.op("vector", lambda v, sl=sl, gp=gp: v.tensor_scalar_mul(out=sl, in0=sl, scalar1=gp))
                    p.op("vector", lambda v, sl=sl, psl=psl: v.tensor_add(out=sl, in0=sl, in1=psl))
                tokcp = p.op("vector", lambda v, pr=pr: v.tensor_copy(
                    out=state_b[:, pr * 64:pr * 64 + 64], in_=state_f[:, pr * 64:pr * 64 + 64]))
                state_cp[pr] = tokcp
                banktok[("st", pr % 2)] = tokcp
                if ATTN_A_PARTS == "blocks1c":
                    continue
                p.wait("scalar", tokr)
                sqd = scr[:, 1112:1176]
                rtf = [scr[:, 1048:1112], scr[:, 1176:1240]]
                colptr[0] = 0
                c_sum = col(); col()
                c_ss = col(); col()
                mean2 = col(); col()
                va2 = col(); col()
                msq2 = col(); col()
                nmean2 = col(); col()
                var2 = col(); col()
                lnv2 = col(); col()
                rstd2 = col(); col()
                nbi2 = col(); col()
                dump2 = col(); col()
                w2c = lambda c: c.broadcast_free(2) if False else None
                def c2(c):
                    return cols[:, c.offs:c.offs + 2] if hasattr(c, 'offs') else c
                tevac = [None, None]
                for half in range(2):
                    tevac[half] = p.op("scalar", lambda s, half=half: s.activation(
                        out=rtf[half], in_=pS4[:, rslc[half]], func=AF.Copy,
                        accum_out=cols[:, 0 + half:1 + half]))
                for half in range(2):
                    p.op("scalar", lambda s, half=half: s.activation(
                        out=sqd, in_=rtf[half], func=AF.Square,
                        accum_out=cols[:, 2 + half:3 + half]))
                CS = lambda a, b: cols[:, a:b]
                p.op("scalar", lambda s: s.activation(
                    out=CS(4, 6), in_=CS(0, 2), func=AF.Copy, scale=1.0 / HD))
                p.op("scalar", lambda s: s.activation(
                    out=CS(6, 8), in_=CS(2, 4), func=AF.Copy, scale=1.0 / HD))
                p.op("scalar", lambda s: s.activation(
                    out=CS(8, 10), in_=CS(4, 6), func=AF.Square))
                p.op("scalar", lambda s: s.activation(
                    out=CS(10, 12), in_=CS(4, 6), func=AF.Copy, scale=-1.0))
                p.op("scalar", lambda s: s.activation(
                    out=CS(24, 26), in_=CS(6, 8), func=AF.Copy))
                for half in range(2):
                    p.op("scalar", lambda s, half=half: s.activation(
                        out=cols[:, 12 + half:13 + half], in_=cols[:, 8 + half:9 + half],
                        func=AF.Identity, scale=-1.0, bias=cols[:, 6 + half:7 + half]))
                p.op("scalar", lambda s: s.activation(
                    out=CS(14, 16), in_=CS(12, 14), func=AF.Ln, bias=eps_col[:]))
                p.op("scalar", lambda s: s.activation(
                    out=CS(16, 18), in_=CS(14, 16), func=AF.Exp, scale=-0.5))
                p.op("scalar", lambda s: s.activation(
                    out=CS(24, 26), in_=CS(10, 12), func=AF.Copy))
                for half in range(2):
                    p.op("scalar", lambda s, half=half: s.activation(
                        out=cols[:, 18 + half:19 + half], in_=cols[:, 16 + half:17 + half],
                        func=AF.Identity, scale=cols[:, 10 + half:11 + half]))
                p.op("scalar", lambda s: s.activation(
                    out=CS(24, 26), in_=CS(18, 20), func=AF.Copy))
                tn = None
                for half in range(2):
                    tn = p.op("scalar", lambda s, half=half, pr=pr: s.activation(
                        out=nb[:, pr * 128 + 64 * half: 64 * half + pr * 128 + 64],
                        in_=rtf[half],
                        func=AF.Identity, scale=cols[:, 16 + half:17 + half],
                        bias=cols[:, 18 + half:19 + half]))
                banktok[3] = tevac[1]
                banktok[0] = tevac[1]
            if ATTN_A_PARTS in ("blocks1a", "blocks1b", "blocks1c"):
                continue
            p.wait("vector", tn)
            p.op("vector", lambda v: v.tensor_mul(out=nb[:, 0:1024], in0=nb[:, 0:1024], in1=gsb[:, 0:1024]))
            p.op("vector", lambda v: v.tensor_copy(out=nbb, in_=nb[:, 0:1024]))
            toknb = p.op("vector", lambda v, b=b: v.tensor_add(
                out=obuf[:, b * D:(b + 1) * D], in0=obuf[:, b * D:(b + 1) * D], in1=nbb))
            p.wait("scalar", toknb)

        if DBG_OBUF:
            p.barrier()
            tokdo = p.dma("sync", dbg_obuf, obuf[:])
            p.wait("sync", tokdo)
        if ATTN_A_PARTS.startswith("blocks"):
            p.barrier(); sctx.close(); return

        tok_blocks = p.now("vector")
        p.wait("tensor", tok_blocks)
        o2fm = sb[:, 0:1024]
        evac_tok = [None, None]
        for tt in range(NT):
            tok_tp = None
            for i in range(8):
                tok_tp = pe_transpose(obuf[:, tt * D + i * 128: tt * D + i * 128 + 128],
                                      o2fm[:, i * 128:(i + 1) * 128])
            p.wait("tensor", tok_tp)
            for o in range(2):
                if evac_tok[o] is not None:
                    p.wait("tensor", evac_tok[o])
                for i in range(8):
                    mt = p.pe(lambda t, o=o, i=i: t.matmul(
                        out=pS[o][:], lhsT=o2fm[:, i * 128:(i + 1) * 128],
                        rhs=w2s[:, i * D + o * 512: i * D + o * 512 + 512],
                        start=(i == 0), stop=(i == 7)), inc=(i == 7))
                p.wait("vector", mt)
                p.op("vector", lambda v, o=o: v.tensor_add(
                    out=scr[:, 1024:1536], in0=pS[o][:], in1=gbw2_b[:, o * 512:o * 512 + 512]))
                xt = x_res[:, tt * D + o * 512: tt * D + o * 512 + 512]
                evac_tok[o] = p.op("vector", lambda v, xt=xt: v.tensor_add(
                    out=xt, in0=xt, in1=scr[:, 1024:1536]))
        p.barrier()
        sctx.close()

    def _ffn(first):
        _ffn_impl(first)

    def _ffn_impl(first):
        sctx = ExitStack()
        A = lambda n, sh, dt: sctx.enter_context(nc.sbuf_tensor(n, sh, dt))
        sfx = "f1" if first else "f2"
        if first:
            _ternarize(sctx, A, sfx)
        _ffn_compute(sctx, A, sfx)
        p.barrier()
        sctx.close()

    def _ternarize(sctx_unused, A_unused, sfx):
        tctx = ExitStack()
        A = lambda n, sh, dt: tctx.enter_context(nc.sbuf_tensor(n, sh, dt))
        wbuf = [A(f"tw{i}_{sfx}", [128, INNER], F32) for i in range(2)]
        dump = A("tdump_" + sfx, [128, INNER], BF16)
        fscr = A("tfscr_" + sfx, [128, INNER], F32)
        stg = A("tstg_" + sfx, [128, 32 * 128], BF16)
        awsum = col()
        c_t = col()
        plans = [(wg_f, 32, D, tern_g, 0), (wu_f, 32, D, tern_u, 1), (wd_f, 8, INNER, tern_d, 2)]
        for (wsrc, nchunk, width, tdst, mwi) in plans:
            colptr[0] = 8
            w_c = wsrc.rearrange("(c q) i -> c q i", q=128)
            p.op("vector", lambda v: v.memset(awsum, 0.0))
            last_use = [None, None]
            last_add = None
            for c in range(nchunk):
                buf = wbuf[c % 2]
                if last_use[c % 2] is not None:
                    p.wait("sync", last_use[c % 2])
                tokd = p.dma("sync", buf[:, 0:width], w_c[c])
                p.wait("scalar", tokd)
                p.wait("scalar", last_add)
                p.op("scalar", lambda s, buf=buf, width=width: s.activation(
                    out=dump[:, 0:width], in_=buf[:, 0:width], func=AF.Abs, accum_out=c_t))
                tth = p.op("scalar", lambda s: s.activation(out=cols[:, 62:63], in_=c_t, func=AF.Copy))
                p.wait("vector", tth)
                last_add = p.op("vector", lambda v: v.tensor_add(out=awsum, in0=awsum, in1=c_t))
                last_use[c % 2] = tth
            tok = p.op("vector", lambda v: v.tensor_copy(out=cols[:, 62:63], in_=awsum))
            p.wait("tensor", tok)
            tokmm = p.pe(lambda t: t.matmul(out=pA[0:1, 0:1], lhsT=awsum, rhs=ones_f[:, 0:1],
                                            start=True, stop=True), inc=True)
            p.wait("vector", tokmm)
            nelem = float(nchunk * 128 * width)
            c_tot = col()
            p.op("vector", lambda v, c_tot=c_tot: v.tensor_copy(out=c_tot[0:1, :], in_=pA[0:1, 0:1]))
            tokt = p.op("vector", lambda v, c_tot=c_tot: v.tensor_copy(out=cols[0:1, 62:63], in_=c_tot[0:1, :]))
            pA_done(tokt)
            p.wait("tensor", tokt)
            tokmm = p.pe(lambda t, c_tot=c_tot: t.matmul(out=pA[:, 0:1], lhsT=ones_f[0:1, :],
                                            rhs=c_tot[0:1, :], start=True, stop=True), inc=True)
            p.wait("vector", tokmm)
            mc_col = col(); negmc_col = col(); sw_col = col()
            p.op("vector", lambda v, nelem=nelem: v.tensor_scalar(
                out=mc_col, in0=pA[:, 0:1], scalar1=1.0 / nelem, scalar2=1e-5,
                op0=ALU.mult, op1=ALU.max))
            p.op("vector", lambda v, mwi=mwi: v.tensor_copy(out=mw_cols[:, mwi:mwi + 1], in_=mc_col))
            p.op("vector", lambda v: v.tensor_scalar_mul(out=negmc_col, in0=mc_col, scalar1=-1.0))
            p.op("vector", lambda v: v.reciprocal(out=sw_col, in_=mc_col))
            p.op("vector", lambda v: v.tensor_add(out=cols[:, 62:63], in0=sw_col, in1=negmc_col))
            tok = p.op("vector", lambda v: v.tensor_add(out=cols[:, 62:63], in0=cols[:, 62:63], in1=mc_col))
            pA_done(tok)
            last_use = [None, None]
            last_stg = None
            ntile = width // 128
            for c in range(nchunk):
                buf = wbuf[c % 2]
                if last_use[c % 2] is not None:
                    p.wait("sync", last_use[c % 2])
                tokd = p.dma("sync", buf[:, 0:width], w_c[c])
                p.wait("vector", tokd)
                p.wait("vector", tok)
                p.op("vector", lambda v, buf=buf, width=width, mc_col=mc_col: v.tensor_scalar_min(
                    out=buf[:, 0:width], in0=buf[:, 0:width], scalar1=mc_col))
                tokg = p.op("vector", lambda v, buf=buf, width=width, negmc_col=negmc_col: v.tensor_scalar_max(
                    out=buf[:, 0:width], in0=buf[:, 0:width], scalar1=negmc_col))
                p.wait("scalar", tokg)
                p.wait("scalar", tok)
                toka = p.op("scalar", lambda s, buf=buf, width=width, sw_col=sw_col: s.activation(
                    out=fscr[:, 0:width], in_=buf[:, 0:width], func=AF.Identity,
                    scale=sw_col, bias=magic_col[:]))
                p.wait("vector", toka)
                p.wait("vector", p.now("tensor"))
                if last_stg is not None:
                    p.wait("vector", last_stg)
                tokt = p.op("vector", lambda v, width=width: v.tensor_scalar_add(
                    out=dump[:, 0:width], in0=fscr[:, 0:width], scalar1=-MAGIC))
                for j in range(ntile):
                    pe_transpose(dump[:, j * 128:(j + 1) * 128], stg[:, j * 128:(j + 1) * 128],
                                 tok_src_v=tokt)
                last_use[c % 2] = toka
                tokv = p.now("vector")
                p.wait("sync", tokv)
                dst = tdst[:, :, c * 128:(c + 1) * 128].rearrange("i q o -> q i o")
                tokso = p.dma("sync", dst, stg[:, 0:ntile * 128].rearrange("q (i o) -> q i o", o=128))
                last_stg = tokso
        p.barrier()
        tctx.close()

    def _ffn_compute(sctx, A, sfx):
        arena = A("harena_" + sfx, [128, 17408], F32)
        hfm = arena[:, 0:16384]
        ab16 = arena[:].bitcast(BF16)
        dfm = arena[:, 9216:13312]
        hq0 = A("hq0_" + sfx, [128, 512], BF16)
        yqf = A("yqf_" + sfx, [128, 8 * 512], BF16)
        yqtm = ab16[:, 0:1024]
        wstr = [A(f"wstr{i}_{sfx}", [128, 2048], BF16) for i in range(2)]
        wdstr = [A(f"wdstr{i}_{sfx}", [128, 4096], BF16) for i in range(2)]
        scrq = arena[:, 16384:17408]
        scrA2 = [A(f"scrA{i}_{sfx}", [128, 512], F32) for i in range(2)]
        scrB2 = [A(f"scrB{i}_{sfx}", [128, 512], F32) for i in range(2)]
        scrC2 = [A(f"scrC{i}_{sfx}", [128, 512], F32) for i in range(2)]
        scrA = scrA2[0]; scrB = scrB2[0]
        hmax = A("hmax_" + sfx, [128, 512], F32)
        hmin = A("hmin_" + sfx, [128, 512], F32)
        bcg = A("bcg_" + sfx, [128, 512], F32)
        bcu = A("bcu_" + sfx, [128, 512], F32)
        bcs2 = A("bcs2_" + sfx, [128, 512], F32)
        bcd = A("bcd_" + sfx, [128, 512], F32)
        rowbA = arena[0:1, 0:1536]
        rowbS = arena[0:1, 16384:17408]
        rowjunk = cols[0:1, 0:8]

        def hq_view(oc):
            return hq0[:] if oc == 0 else ab16[:, oc * 512:(oc + 1) * 512]

        wtok = [None, None]
        wtokd = [None, None]
        tokd_arr = [None, None]
        tokdd = [None, None]

        for gi in range(4):
            if gi > 0:
                p.wait("scalar", p.now("vector"))
                p.wait("scalar", p.now("tensor"))
            colptr[0] = 0
            sts = []
            for tsub in range(4):
                tt = gi * 4 + tsub
                st = ln_stats(x_tile(tt), True, scrq[:, 0:1024])
                sts.append(st)
                p.wait("scalar", st["tok"])
                ta = p.op("scalar", lambda s, tt=tt, st=st: s.activation(
                    out=scrq[:, 0:1024], in_=x_tile(tt), func=AF.Identity,
                    scale=st["scale"], bias=st["bias"]))
                p.wait("vector", ta)
                tq = p.op("vector", lambda v: v.tensor_scalar(
                    out=yqtm, in0=scrq[:, 0:1024], scalar1=MAGIC, scalar2=-MAGIC,
                    op0=ALU.add, op1=ALU.add))
                for i in range(8):
                    pe_transpose(yqtm[:, i * 128:(i + 1) * 128],
                                 yqf[:, i * 512 + tsub * 128: i * 512 + tsub * 128 + 128],
                                 tok_src_v=tq)
                p.wait("scalar", p.now("tensor"))
                p.wait("vector", p.now("tensor"))
            tok_yq = p.now("vector")
            for tsub in range(4):
                tok_tr = pe_transpose(sts[tsub]["rcp_s"], rowbA[0:1, tsub * 128:tsub * 128 + 128],
                                      f32=True, tok_src_v=sts[tsub]["tok"])
            p.wait("vector", tok_tr)
            p.op("vector", lambda v: v.tensor_scalar(
                out=rowbA[0:1, 512:1024], in0=rowbA[0:1, 0:512],
                scalar1=mw_cols[0:1, 0:1], scalar2=None, op0=ALU.mult))
            p.op("vector", lambda v: v.tensor_scalar(
                out=rowbA[0:1, 1024:1536], in0=rowbA[0:1, 0:512],
                scalar1=mw_cols[0:1, 1:2], scalar2=None, op0=ALU.mult))
            tok = p.op("vector", lambda v: v.tensor_add(
                out=rowjunk, in0=rowbA[0:1, 512:520], in1=rowbA[0:1, 1024:1032]))
            p.wait("tensor", tok)
            tk1 = p.pe(lambda t: t.matmul(out=pB[:], lhsT=ones_f[0:1, :],
                                          rhs=rowbA[0:1, 512:1024], start=True, stop=True), inc=True)
            p.wait("vector", tk1)
            tokbc = p.op("vector", lambda v: v.tensor_copy(out=bcg[:], in_=pB[:]))
            p.wait("tensor", tokbc)
            tk1 = p.pe(lambda t: t.matmul(out=pB[:], lhsT=ones_f[0:1, :],
                                          rhs=rowbA[0:1, 1024:1536], start=True, stop=True), inc=True)
            p.wait("vector", tk1)
            tokbc = p.op("vector", lambda v: v.tensor_copy(out=bcu[:], in_=pB[:]))
            p.wait("tensor", tok_yq)
            p.wait("tensor", tokbc)
            ptok = [None, None]
            siltok = [None, None]
            htok = [None, None]
            prevh = None
            tok_hm = None

            def issue_gu_dma(oc_):
                par_ = oc_ % 2
                p.wait("sync", wtok[par_])
                gsl = tern_g[:, :, oc_ * 128:(oc_ + 1) * 128].rearrange("i q o -> q i o")
                usl = tern_u[:, :, oc_ * 128:(oc_ + 1) * 128].rearrange("i q o -> q i o")
                p.dma("sync", wstr[par_][:, 0:1024].rearrange("q (i o) -> q i o", o=128), gsl)
                tokd_arr[par_] = p.dma("sync", wstr[par_][:, 1024:2048].rearrange("q (i o) -> q i o", o=128), usl)
            issue_gu_dma(0)
            issue_gu_dma(1)

            def lagged_max(oc_, tokh_):
                p.wait("vector", tokh_)
                if oc_ == 0:
                    p.op("vector", lambda v: v.tensor_copy(out=hmax[:], in_=hfm[:, 0:512]))
                    return p.op("vector", lambda v: v.tensor_copy(out=hmin[:], in_=hfm[:, 0:512]))
                p.op("vector", lambda v, oc_=oc_: v.tensor_tensor(
                    out=hmax[:], in0=hmax[:], in1=hfm[:, oc_ * 512:(oc_ + 1) * 512], op=ALU.max))
                return p.op("vector", lambda v, oc_=oc_: v.tensor_tensor(
                    out=hmin[:], in0=hmin[:], in1=hfm[:, oc_ * 512:(oc_ + 1) * 512], op=ALU.min))

            for oc in range(32):
                par = oc % 2
                tokd = tokd_arr[par]
                p.wait("tensor", tokd)
                if ptok[par] is not None:
                    p.wait("tensor", ptok[par])
                for i in range(8):
                    p.pe(lambda t, par=par, i=i: t.matmul(
                        out=pS[2 * par], lhsT=wstr[par][:, i * 128:(i + 1) * 128],
                        rhs=yqf[:, i * 512:(i + 1) * 512], start=(i == 0), stop=(i == 7)))
                    mt = p.pe(lambda t, par=par, i=i: t.matmul(
                        out=pS[2 * par + 1], lhsT=wstr[par][:, 1024 + i * 128:1024 + i * 128 + 128],
                        rhs=yqf[:, i * 512:(i + 1) * 512], start=(i == 0), stop=(i == 7)),
                        inc=(i == 7))
                wtok[par] = mt
                if oc + 2 < 32:
                    p.wait("sync", mt)
                    issue_gu_dma(oc + 2)
                p.wait("vector", mt)
                p.wait("vector", siltok[par])
                p.op("vector", lambda v, par=par: v.tensor_mul(out=scrA2[par][:], in0=pS[2 * par], in1=bcg[:]))
                p.wait("vector", htok[par])
                tokup = p.op("vector", lambda v, par=par: v.tensor_mul(out=scrC2[par][:], in0=pS[2 * par + 1], in1=bcu[:]))
                ptok[par] = tokup
                p.wait("scalar", tokup)
                p.wait("scalar", htok[par])
                toksl = p.op("scalar", lambda s, par=par: s.activation(out=scrB2[par][:], in_=scrA2[par][:], func=AF.Silu))
                siltok[par] = toksl
                p.wait("gpsimd", toksl)
                tokh = p.op("gpsimd", lambda gq, oc=oc, par=par: gq.tensor_mul(
                    out=hfm[:, oc * 512:(oc + 1) * 512], in0=scrB2[par][:], in1=scrC2[par][:]))
                htok[par] = tokh
                if prevh is not None:
                    tok_hm = lagged_max(*prevh)
                prevh = (oc, tokh)
            tok_hm = lagged_max(*prevh)
            p.wait("vector", tok_hm)
            p.op("vector", lambda v: v.tensor_scalar_mul(out=hmin[:], in0=hmin[:], scalar1=-1.0))
            tok_hm = p.op("vector", lambda v: v.tensor_max(out=hmax[:], in0=hmax[:], in1=hmin[:]))
            p.wait("tensor", tok_hm)
            c_t = cols[:, 60:61]
            for tsub in range(4):
                tok_tr = pe_transpose(hmax[:, tsub * 128:tsub * 128 + 128],
                                      None, f32=True, evac=False)
                p.wait("vector", tok_tr)
                tokr = p.op("vector", lambda v: v.reduce_max(
                    out=cols[:, 58:59], in_=pA[0:128, 0:128], axis=AX.X))
                pA_done(tokr)
                p.op("vector", lambda v: v.tensor_scalar_max(out=cols[:, 58:59], in0=cols[:, 58:59], scalar1=1e-5))
                p.op("vector", lambda v: v.reciprocal(out=cols[:, 59:60], in_=cols[:, 58:59]))
                p.op("vector", lambda v: v.tensor_scalar_mul(out=cols[:, 59:60], in0=cols[:, 59:60], scalar1=127.0))
                p.op("vector", lambda v: v.tensor_scalar_mul(out=cols[:, 58:59], in0=cols[:, 58:59], scalar1=1.0 / 127.0))
                tok = p.op("vector", lambda v, c_t=c_t: v.tensor_add(out=c_t, in0=cols[:, 58:59], in1=cols[:, 59:60]))
                pe_transpose(cols[:, 59:60], rowbS[0:1, tsub * 128:tsub * 128 + 128],
                             f32=True, tok_src_v=tok)
                tok_t2 = pe_transpose(cols[:, 58:59], rowbS[0:1, 512 + tsub * 128:512 + tsub * 128 + 128],
                                      f32=True, tok_src_v=tok)
            p.wait("vector", tok_t2)
            tokr = p.op("vector", lambda v: v.tensor_scalar(
                out=rowbS[0:1, 512:1024], in0=rowbS[0:1, 512:1024],
                scalar1=mw_cols[0:1, 2:3], scalar2=None, op0=ALU.mult))
            tok = p.op("vector", lambda v: v.tensor_add(
                out=rowjunk, in0=rowbS[0:1, 0:8], in1=rowbS[0:1, 512:520]))
            p.wait("tensor", tok)
            tk1 = p.pe(lambda t: t.matmul(out=pB[:], lhsT=ones_f[0:1, :],
                                          rhs=rowbS[0:1, 0:512], start=True, stop=True), inc=True)
            p.wait("vector", tk1)
            tokbc = p.op("vector", lambda v: v.tensor_copy(out=bcs2[:], in_=pB[:]))
            p.wait("tensor", tokbc)
            tk1 = p.pe(lambda t: t.matmul(out=pB[:], lhsT=ones_f[0:1, :],
                                          rhs=rowbS[0:1, 512:1024], start=True, stop=True), inc=True)
            p.wait("vector", tk1)
            tokbc = p.op("vector", lambda v: v.tensor_copy(out=bcd[:], in_=pB[:]))
            toklast = None
            for oc in range(32):
                p.wait("gpsimd", tokbc)
                tokm = p.op("gpsimd", lambda gq, oc=oc: gq.tensor_mul(
                    out=scrA[:] if oc % 2 == 0 else scrB[:],
                    in0=hfm[:, oc * 512:(oc + 1) * 512], in1=bcs2[:]))
                p.wait("vector", tokm)
                toklast = p.op("vector", lambda v, oc=oc: v.tensor_scalar(
                    out=hq_view(oc), in0=scrA[:] if oc % 2 == 0 else scrB[:],
                    scalar1=MAGIC, scalar2=-MAGIC, op0=ALU.add, op1=ALU.add))
                p.wait("gpsimd", toklast)
            if DBG_FFN and gi == 0 and sfx == "f1":
                p.barrier()
                p.dma("sync", dbg_fa, arena[:])
                p.dma("sync", dbg_fy, yqf[:])
                p.dma("sync", dbg_fb[:, 0:512], bcg[:])
                p.dma("sync", dbg_fb[:, 512:1024], bcu[:])
                p.dma("sync", dbg_fb[:, 1024:1536], bcs2[:])
                p.dma("sync", dbg_fb[:, 1536:2048], bcd[:])
                p.dma("sync", dbg_fr[0:1, 0:1024], rowbS[0:1, 0:1024])
                tokdf = p.dma("sync", dbg_fm[0:1, 0:4], mw_cols[0:1, 0:4])
                p.wait("sync", tokdf)
            p.wait("tensor", toklast)
            p.wait("tensor", tokbc)
            dtok = [None, None]

            def issue_d_dma(cc_):
                p.wait("sync", wtokd[cc_ % 2])
                dsl = tern_d[:, :, cc_ * 128:(cc_ + 1) * 128].rearrange("i q o -> q i o")
                tokdd[cc_ % 2] = p.dma("sync", wdstr[cc_ % 2][:, 0:4096].rearrange("q (i o) -> q i o", o=128), dsl)
            issue_d_dma(0)
            issue_d_dma(1)
            for cc in range(8):
                par = cc % 2
                tokd = tokdd[par]
                p.wait("tensor", tokd)
                if dtok[par] is not None:
                    p.wait("tensor", dtok[par])
                for j in range(32):
                    mt = p.pe(lambda t, par=par, j=j: t.matmul(
                        out=pS[2 * par], lhsT=wdstr[par][:, j * 128:(j + 1) * 128],
                        rhs=hq_view(j), start=(j == 0), stop=(j == 31)), inc=(j == 31))
                wtokd[par] = mt
                if cc + 2 < 8:
                    p.wait("sync", mt)
                    issue_d_dma(cc + 2)
                p.wait("vector", mt)
                tokdf = p.op("vector", lambda v, cc=cc, par=par: v.tensor_mul(
                    out=dfm[:, cc * 512:(cc + 1) * 512], in0=pS[2 * par], in1=bcd[:]))
                dtok[par] = tokdf
            p.wait("tensor", dtok[0]); p.wait("tensor", dtok[1])
            for cb in range(8):
                for tsub in range(4):
                    tok_tr = pe_transpose(dfm[:, cb * 512 + tsub * 128: cb * 512 + tsub * 128 + 128],
                                          None, f32=True, evac=False)
                    p.wait("vector", tok_tr)
                    xt = x_res[:, (gi * 4 + tsub) * D + cb * 128: (gi * 4 + tsub) * D + cb * 128 + 128]
                    tokx = p.op("vector", lambda v, xt=xt: v.tensor_add(out=xt, in0=xt, in1=pA[0:128, 0:128]))
                    pA_done(tokx)
            p.wait("tensor", tokx)

    def _attn_b():
        sctx = ExitStack()
        A = lambda n, sh, dt: sctx.enter_context(nc.sbuf_tensor(n, sh, dt))
        qrope = A("qrope", [128, 8 * T], BF16)
        krope = A("krope", [128, 8 * T], BF16)
        sub1 = ExitStack()
        A1 = lambda n, sh, dt: sub1.enter_context(nc.sbuf_tensor(n, sh, dt))
        y3fm = A1("y3fm", [128, 8 * T], BF16)
        subv = ExitStack()
        Av = lambda n, sh, dt: subv.enter_context(nc.sbuf_tensor(n, sh, dt))
        wvs = Av("wvs", [128, 8 * D], BF16)
        vstage = Av("vstage", [128, NH * 65], BF16)
        wtmp = Av("wtmp_b", [128, D], BF16)
        sbq = qrope[:, 0:1024]
        scrq = qrope[:, 2048:4096].bitcast(F32)

        for tt in range(NT):
            colptr[0] = 0
            st = ln_stats(x_tile(tt), False, scrq[:, 0:1024])
            p.wait("scalar", st["tok"])
            ty = p.op("scalar", lambda s, tt=tt, st=st: s.activation(
                out=sbq[:], in_=x_tile(tt), func=AF.Identity,
                scale=st["scale"], bias=st["bias"]))
            for i in range(8):
                pe_transpose(sbq[:, i * 128:(i + 1) * 128],
                             y3fm[:, i * T + tt * 128: i * T + tt * 128 + 128],
                             tok_src_v=ty)
            p.wait("scalar", p.now("tensor"))
        tok_y3 = p.now("vector")

        Wv_c = Wv_b.rearrange("(c q) i -> c q i", q=128)
        for oc in range(8):
            tokd = p.dma("gpsimd", wtmp[:], Wv_c[oc])
            p.wait("tensor", tokd)
            for i in range(8):
                pe_transpose(wtmp[:, i * 128:(i + 1) * 128],
                             wvs[:, i * D + oc * 128: i * D + oc * 128 + 128])
            p.wait("gpsimd", p.now("tensor"))

        p.wait("tensor", tok_y3)
        ev = [None, None]
        tokvs = None
        for tt in range(NT):
            if tokvs is not None:
                p.wait("vector", tokvs)
            tkm = p.op("vector", lambda v: v.memset(
                vstage[:].rearrange("q (h c) -> q h c", c=65)[:, :, 64:65], 1.0))
            for o in range(2):
                if ev[o] is not None:
                    p.wait("tensor", ev[o])
                mt = None
                for i in range(8):
                    mt = p.pe(lambda t, tt=tt, o=o, i=i: t.matmul(
                        out=pS[o][:], lhsT=y3fm[:, i * T + tt * 128: i * T + tt * 128 + 128],
                        rhs=wvs[:, i * D + o * 512: i * D + o * 512 + 512],
                        start=(i == 0), stop=(i == 7)), inc=(i == 7))
                p.wait("vector", mt)
                dview = vstage[:].rearrange("q (h c) -> q h c", c=65)[:, o * 8:(o + 1) * 8, 0:64]
                sview = pS[o][:].rearrange("q (h c) -> q h c", c=64)
                ev[o] = p.op("vector", lambda v, dview=dview, sview=sview: v.tensor_copy(
                    out=dview, in_=sview))
            p.wait("sync", ev[1])
            tokvs = p.dma("sync", v_scr[tt * 128:(tt + 1) * 128, :], vstage[:])

        subv.close()
        subq = ExitStack()
        Aq = lambda n, sh, dt: subq.enter_context(nc.sbuf_tensor(n, sh, dt))
        wtmp = Aq("wtmpq_b", [128, D], BF16)
        wcol = Aq("wcol_b", [128, D], BF16)
        wrot = Aq("wrot_b", [128, D], BF16)
        scrA = Aq("scrA_b", [128, 512], F32)
        scrB = Aq("scrB_b", [128, 512], F32)
        ropeC = Aq("ropeC_t", [128, T], F32)
        ropeS = Aq("ropeS_t", [128, T], F32)
        p.dma("sync", ropeC[:], ropeC_in)
        tok_rt = p.dma("sync", ropeS[:], ropeS_in)
        p.wait("vector", tok_rt)
        bankt = {}
        for wsrc, dest in ((Wq_b, qrope), (Wk_b, krope)):
            W_c = wsrc.rearrange("(c q) i -> c q i", q=128)
            for oc in range(8):
                tokd = p.dma("gpsimd", wtmp[:], W_c[oc])
                p.wait("tensor", tokd)
                tok_tw = None
                for i in range(8):
                    tok_tw = pe_transpose(wtmp[:, i * 128:(i + 1) * 128],
                                          wcol[:, i * 128:(i + 1) * 128])
                p.wait("gpsimd", p.now("tensor"))
                p.wait("vector", tok_tw)
                p.op("vector", lambda v: v.tensor_scalar_mul(
                    out=wrot[:, 0:1024:2], in0=wcol[:, 1:1024:2], scalar1=-1.0))
                tokrot = p.op("vector", lambda v: v.tensor_copy(
                    out=wrot[:, 1:1024:2], in_=wcol[:, 0:1024:2]))
                p.wait("tensor", tokrot)
                for tgp in range(2):
                    for bk in range(4):
                        if bankt.get(bk) is not None:
                            p.wait("tensor", bankt[bk])
                    mt = None
                    for i in range(8):
                        for tg2 in range(2):
                            tg = 2 * tgp + tg2
                            p.pe(lambda t, i=i, tg=tg, tg2=tg2: t.matmul(
                                out=pS[tg2][:], lhsT=wcol[:, i * 128:(i + 1) * 128],
                                rhs=y3fm[:, i * T + tg * 512: i * T + tg * 512 + 512],
                                start=(i == 0), stop=(i == 7)))
                            mt = p.pe(lambda t, i=i, tg=tg, tg2=tg2: t.matmul(
                                out=pS[2 + tg2][:], lhsT=wrot[:, i * 128:(i + 1) * 128],
                                rhs=y3fm[:, i * T + tg * 512: i * T + tg * 512 + 512],
                                start=(i == 0), stop=(i == 7)),
                                inc=(i == 7 and tg2 == 1))
                    p.wait("vector", mt)
                    for tg2 in range(2):
                        tg = 2 * tgp + tg2
                        tsl = slice(tg * 512, tg * 512 + 512)
                        p.op("vector", lambda v, tg2=tg2, tsl=tsl: v.tensor_mul(
                            out=scrA[:], in0=pS[tg2][:], in1=ropeC[:, tsl]))
                        p.op("vector", lambda v, tg2=tg2, tsl=tsl: v.tensor_mul(
                            out=scrB[:], in0=pS[2 + tg2][:], in1=ropeS[:, tsl]))
                        tokev = p.op("vector", lambda v, dest=dest, oc=oc, tg=tg: v.tensor_add(
                            out=dest[:, oc * T + tg * 512: oc * T + tg * 512 + 512],
                            in0=scrA[:], in1=scrB[:]))
                        bankt[tg2] = tokev
                        bankt[2 + tg2] = tokev
        p.barrier()
        subq.close()
        sub1.close()

        sub2 = ExitStack()
        A2 = lambda n, sh, dt: sub2.enter_context(nc.sbuf_tensor(n, sh, dt))
        wos2 = A2("wos2", [64, NH * 1024], BF16)
        abuf = A2("abuf", [64, NH * 512], BF16)
        vhb = [A2("vhb0", [128, NT * 65], BF16), A2("vhb1", [128, NT * 65], BF16)]
        pring = A2("pring", [128, 2048], BF16)
        dsb = pring[:].bitcast(F32)[:, 0:512]
        wtmp2 = vhb[0][:, 0:1024]
        bcf = pring[:].bitcast(F32)[0:64, 512:1024]
        tch = cols[:, 61:62]
        Wo_c = Wo_b.rearrange("(c q) o -> c q o", q=128)
        for cc in range(8):
            tokd = p.dma("gpsimd", wtmp2[:], Wo_c[cc])
            p.wait("tensor", tokd)
            for h in range(NH):
                pe_transpose(wtmp2[:, h * 64:(h + 1) * 64],
                             wos2[0:64, h * 1024 + cc * 128: h * 1024 + cc * 128 + 128])
            p.wait("gpsimd", p.now("tensor"))
        tok_wo = p.now("vector")
        p.wait("tensor", tok_wo)

        vtok = [None, None]
        aev = None
        pvt = None
        texp = None
        pvt2 = [None, None]
        texp2 = [None, None]
        for qg in range(4):
            for h in range(NH):
                pr, half = h // 2, h % 2
                if vtok[h % 2] is not None:
                    p.wait("gpsimd", vtok[h % 2])
                vsl = v_scr[:, 65 * h: 65 * h + 65].rearrange("(kt q) c -> q kt c", q=128)
                tokv = p.dma("gpsimd", vhb[h % 2][:].rearrange("q (kt c) -> q kt c", c=65), vsl)
                p.wait("tensor", tokv)
                for kt2 in range(8):
                    ph = kt2 % 2
                    smt = None
                    if texp2[ph] is not None:
                        p.wait("tensor", texp2[ph])
                    for k2 in range(2):
                        kt = kt2 * 2 + k2
                        smt = p.pe(lambda t, pr=pr, half=half, kt=kt, k2=k2, ph=ph, qg=qg: t.matmul(
                            out=pS4[:, ph * 1024 + k2 * 512: ph * 1024 + k2 * 512 + 512],
                            lhsT=krope[64 * half:64 * half + 64, pr * T + kt * 128: pr * T + kt * 128 + 128],
                            rhs=qrope[64 * half:64 * half + 64, pr * T + qg * 512: pr * T + qg * 512 + 512],
                            start=True, stop=True), inc=(k2 == 1))
                    p.wait("scalar", smt)
                    if pvt2[ph] is not None:
                        p.wait("scalar", pvt2[ph])
                    p.wait("scalar", aev)
                    texp2[ph] = p.op("scalar", lambda s, ph=ph: s.activation(
                        out=pring[:, ph * 1024: ph * 1024 + 1024],
                        in_=pS4[:, ph * 1024: ph * 1024 + 1024],
                        func=AF.Exp, scale=float(HD) ** -0.5))
                    texp = texp2[ph]
                    p.wait("tensor", texp)
                    for k2 in range(2):
                        kt = kt2 * 2 + k2
                        pvt2[ph] = p.pe(lambda t, h=h, kt=kt, k2=k2, ph=ph: t.matmul(
                            out=pB[0:65, :], lhsT=vhb[h % 2][:, kt * 65: kt * 65 + 65],
                            rhs=pring[:, ph * 1024 + k2 * 512: ph * 1024 + k2 * 512 + 512],
                            start=(kt == 0), stop=(kt == 15)), inc=(k2 == 1))
                    pvt = pvt2[ph]
                vtok[h % 2] = pvt
                p.wait("vector", pvt)
                p.op("vector", lambda v: v.reciprocal(out=dsb[64:65, :], in_=pB[64:65, :]))
                tokt = p.op("vector", lambda v: v.reduce_sum(out=tch[64:65, :], in_=dsb[64:65, :], axis=AX.X))
                p.wait("tensor", tokt)
                if tp_state["lastA"] is not None:
                    p.wait("tensor", tp_state["lastA"])
                tokbc = p.pe(lambda t: t.matmul(out=pA[0:64, :], lhsT=ones_f[64:65, 0:64],
                                                rhs=dsb[64:65, :], start=True, stop=True), inc=True)
                p.wait("vector", tokbc)
                tokb = p.op("vector", lambda v: v.tensor_copy(out=bcf[0:64, :], in_=pA[0:64, :]))
                pA_done(tokb)
                aev = p.op("vector", lambda v, h=h: v.tensor_mul(
                    out=abuf[0:64, h * 512:(h + 1) * 512], in0=pB[0:64, :], in1=bcf[0:64, :]))
                p.wait("tensor", aev)
            woev = [None, None]
            for qsub in range(4):
                for ct in range(2):
                    bk = (qsub * 2 + ct) % 2
                    if woev[bk] is not None:
                        p.wait("tensor", woev[bk])
                    mt = None
                    for h in range(NH):
                        mt = p.pe(lambda t, h=h, qsub=qsub, ct=ct, bk=bk: t.matmul(
                            out=pS[bk][:],
                            lhsT=abuf[0:64, h * 512 + qsub * 128: h * 512 + qsub * 128 + 128],
                            rhs=wos2[0:64, h * 1024 + ct * 512: h * 1024 + ct * 512 + 512],
                            start=(h == 0), stop=(h == NH - 1)), inc=(h == NH - 1))
                    p.wait("vector", mt)
                    xt = x_res[:, (qg * 4 + qsub) * D + ct * 512: (qg * 4 + qsub) * D + ct * 512 + 512]
                    woev[bk] = p.op("vector", lambda v, xt=xt, bk=bk: v.tensor_add(
                        out=xt, in0=xt, in1=pS[bk][:]))
            p.wait("tensor", woev[0]); p.wait("tensor", woev[1])
        p.barrier()
        sub2.close()
        sctx.close()

    p.wait("vector", tok_x)
    p.barrier()
    stages = [stage_attn_a, lambda: stage_ffn(True), stage_attn_b, lambda: stage_ffn(False)]
    for i in range(nstages):
        stages[i]()
        if i in dbg:
            tok = p.dma("sync", dbg[i].rearrange("(tt q) d -> q tt d", q=128),
                        x_res[:].rearrange("q (tt d) -> q tt d", d=D))
            p.wait("sync", tok)
    p.barrier()
    tok = p.dma("sync", x_out.rearrange("(tt q) d -> q tt d", q=128),
                x_res[:].rearrange("q (tt d) -> q tt d", d=D))
    p.wait("sync", tok)
    p.realize()
    ctx.close()
    return nc


def kernel(L0_attn_a_x, L0_attn_a_wq, L0_attn_a_wk, L0_attn_a_wv, L0_attn_a_w1,
           L0_attn_a_w2, L0_attn_a_gn_scale, L0_attn_a_gn_bias, L0_attn_b_Wq,
           L0_attn_b_Wk, L0_attn_b_Wv, L0_attn_b_Wo, L0_ffn_w_gate,
           L0_ffn_w_up, L0_ffn_w_down, _nstages=4, _debug_taps=()):
    x = np.ascontiguousarray(np.asarray(L0_attn_a_x, dtype=np.float32))
    consts = host_constants()
    nc = build(nstages=_nstages, debug_taps=_debug_taps)
    F = lambda a: np.ascontiguousarray(np.asarray(a, dtype=np.float32))
    shared = {
        "wq_a": F(L0_attn_a_wq), "wk_a": F(L0_attn_a_wk), "wv_a": F(L0_attn_a_wv),
        "w1_a": F(L0_attn_a_w1), "w2_a": F(L0_attn_a_w2),
        "gns": F(L0_attn_a_gn_scale), "gnb": F(L0_attn_a_gn_bias),
        "Wq_b": F(L0_attn_b_Wq), "Wk_b": F(L0_attn_b_Wk),
        "Wv_b": F(L0_attn_b_Wv), "Wo_b": F(L0_attn_b_Wo),
        "wg_f": F(L0_ffn_w_gate), "wu_f": F(L0_ffn_w_up), "wd_f": F(L0_ffn_w_down),
        "identc": consts["ident"], "triTc": consts["triT"],
        "qsc": consts["qs"], "knc": consts["kn"], "kpc": consts["kp"],
        "ropeCc": consts["ropeC"], "ropeSc": consts["ropeS"],
    }
    in_maps = [dict(shared, x_in=np.ascontiguousarray(x[c])) for c in range(8)]
    res = run_bass_kernel_spmd(nc, in_maps, core_ids=list(range(8)))
    out = np.stack([np.asarray(r["x_out"], dtype=np.float32) for r in res.results])
    if _debug_taps:
        return out, [np.stack([np.asarray(r[f"dbg{i}"], np.float32) for r in res.results]) for i in _debug_taps]
    return out


# version 09
# speedup vs baseline: 1.0018x; 1.0018x over previous
import numpy as np
import ml_dtypes
import concourse.bass as bass
import concourse.mybir as mybir
from concourse.bass_utils import run_bass_kernel_spmd
from contextlib import ExitStack

F32 = mybir.dt.float32
BF16 = mybir.dt.bfloat16
AF = mybir.ActivationFunctionType
ALU = mybir.AluOpType
AX = mybir.AxisListType

T = 2048
D = 1024
NH = 16
HD = 64
INNER = 4096
NT = T // 128
NPAIR = NH // 2
MAGIC = float(np.float32(1.5 * 2.0 ** 23))
EPS = 1e-5
ENGS = ("sync", "gpsimd", "vector", "scalar", "tensor")
ATTN_A_PARTS = "full"
DBG_OBUF = False
DBG_FFN = False


def _gammas():
    lin = np.linspace(np.log(1.0 / 512.0), np.log(1.0 / 32.0), NH)
    return 1.0 - np.exp(lin)


def host_constants():
    g = _gammas()
    c = {}
    c["ident"] = np.eye(128).astype(ml_dtypes.bfloat16)
    c["triT"] = (np.arange(128)[:, None] <= np.arange(128)[None, :]).astype(ml_dtypes.bfloat16)
    qs = np.zeros((NPAIR, 128, 128), np.float64)
    kn = np.zeros((NPAIR, 128, 128), np.float64)
    kp = np.zeros((NPAIR, 128, 128), np.float64)
    s_ = np.arange(128, dtype=np.float64)
    for pr in range(NPAIR):
        for half in range(2):
            gh = g[2 * pr + half]
            qs[pr, 64 * half:64 * half + 64, :] = gh ** s_[None, :]
            kn[pr, 64 * half:64 * half + 64, :] = gh ** (-s_[None, :])
            kp[pr, :, 64 * half:64 * half + 64] = (gh ** (128.0 - s_))[:, None]
    c["qs"] = qs.astype(ml_dtypes.bfloat16)
    c["kn"] = kn.astype(ml_dtypes.bfloat16)
    c["kp"] = kp.astype(ml_dtypes.bfloat16)
    inv = 1.0 / 10000.0 ** (np.arange(0, HD, 2, dtype=np.float64) / HD)
    fr = np.arange(T, dtype=np.float64)[:, None] * inv[None, :]
    cos, sin = np.cos(fr).T, np.sin(fr).T
    C = np.zeros((128, T), np.float64)
    Sn = np.zeros((128, T), np.float64)
    for half in range(2):
        for j in range(32):
            C[64 * half + 2 * j] = cos[j]; C[64 * half + 2 * j + 1] = cos[j]
            Sn[64 * half + 2 * j] = sin[j]; Sn[64 * half + 2 * j + 1] = sin[j]
    c["ropeC"] = C.astype(np.float32)
    c["ropeS"] = Sn.astype(np.float32)
    return c


class Prog:
    def __init__(self, nc):
        self.nc = nc
        self.prog = []
        self.cnt = {e: 0 for e in ENGS}
        self.dma_cnt = {"sync": 0, "gpsimd": 0}
        self.sems = {}
        self.last_wait = {}

    def op(self, eng, fn):
        assert eng in ("vector", "scalar", "gpsimd")
        n = self.cnt[eng]
        sem = self.sems["c_" + eng]

        def thunk(e, fn=fn, n=n, sem=sem):
            if n > 0:
                e.wait_ge(sem, n)
            fn(e).then_inc(sem, 1)
        self.prog.append((eng, thunk))
        self.cnt[eng] = n + 1
        return (eng, n + 1)

    def pe(self, fn, inc=False):
        if inc:
            sem = self.sems["c_tensor"]

            def thunk(e, fn=fn, sem=sem):
                fn(e).then_inc(sem, 1)
            self.prog.append(("tensor", thunk))
            self.cnt["tensor"] += 1
            return ("tensor", self.cnt["tensor"])
        self.prog.append(("tensor", lambda e, fn=fn: fn(e)))
        return None

    def dma(self, queue, out, in_, **kw):
        self.dma_cnt[queue] += 16
        sem = self.sems["d_" + queue]

        def thunk(e, out=out, in_=in_, kw=kw, sem=sem):
            e.dma_start(out=out, in_=in_, **kw).then_inc(sem, 16)
        self.prog.append((queue, thunk))
        return ("d" + queue, self.dma_cnt[queue])

    def wait(self, eng, token):
        if token is None:
            return
        kind, val = token
        sem_name = ("d_" + kind[1:]) if kind.startswith("d") else ("c_" + kind)
        sem = self.sems[sem_name]
        key = (eng, sem_name)
        if self.last_wait.get(key, 0) >= val:
            return
        self.last_wait[key] = val

        def thunk(e, sem=sem, val=val):
            e.wait_ge(sem, val)
        self.prog.append((eng, thunk))

    def now(self, eng):
        return (eng, self.cnt[eng]) if self.cnt[eng] else None

    def dnow(self, queue):
        return ("d" + queue, self.dma_cnt[queue]) if self.dma_cnt[queue] else None

    def barrier(self):
        toks = [t for t in (self.now("vector"), self.now("scalar"),
                            self.now("gpsimd"), self.now("tensor"),
                            self.dnow("sync"), self.dnow("gpsimd")) if t]
        for e in ENGS:
            for t in toks:
                self.wait(e, t)

    def realize(self):
        prog = self.prog
        with self.nc.Block() as block:
            def runner(name):
                def body(e):
                    for eng, fn in prog:
                        if eng == name:
                            fn(e)
                return body
            block.sync(runner("sync"))
            block.gpsimd(runner("gpsimd"))
            block.vector(runner("vector"))
            block.scalar(runner("scalar"))
            block.tensor(runner("tensor"))


def build(nstages=4, debug_taps=()):
    nc = bass.Bass("TRN2", target_bir_lowering=False, debug=False)
    p = Prog(nc)
    ctx = ExitStack()

    DT = lambda n, sh, dt=F32, kind="ExternalInput": nc.dram_tensor(n, sh, dt, kind=kind).ap()
    x_in = DT("x_in", [T, D])
    x_out = DT("x_out", [T, D], F32, "ExternalOutput")
    dbg = {i: DT(f"dbg{i}", [T, D], F32, "ExternalOutput") for i in debug_taps}
    wq_a = DT("wq_a", [NH, HD, HD]); wk_a = DT("wk_a", [NH, HD, HD]); wv_a = DT("wv_a", [NH, HD, HD])
    w1_a = DT("w1_a", [D, D]); w2_a = DT("w2_a", [D, D])
    gns = DT("gns", [D]); gnb = DT("gnb", [D])
    Wq_b = DT("Wq_b", [D, D]); Wk_b = DT("Wk_b", [D, D]); Wv_b = DT("Wv_b", [D, D]); Wo_b = DT("Wo_b", [D, D])
    wg_f = DT("wg_f", [INNER, D]); wu_f = DT("wu_f", [INNER, D]); wd_f = DT("wd_f", [D, INNER])
    ident_in = DT("identc", [128, 128], BF16)
    triT_in = DT("triTc", [128, 128], BF16)
    qs_in = DT("qsc", [NPAIR, 128, 128], BF16)
    kn_in = DT("knc", [NPAIR, 128, 128], BF16)
    kp_in = DT("kpc", [NPAIR, 128, 128], BF16)
    ropeC_in = DT("ropeCc", [128, T], F32)
    ropeS_in = DT("ropeSc", [128, T], F32)
    dbg_obuf = DT("dbg_obuf", [128, NT * D], BF16, "ExternalOutput") if DBG_OBUF else None
    dbg_fa = DT("dbg_fa", [128, 17408], F32, "ExternalOutput") if DBG_FFN else None
    dbg_fy = DT("dbg_fy", [128, 4096], BF16, "ExternalOutput") if DBG_FFN else None
    dbg_fb = DT("dbg_fb", [128, 2048], F32, "ExternalOutput") if DBG_FFN else None
    dbg_fr = DT("dbg_fr", [1, 1536], F32, "ExternalOutput") if DBG_FFN else None
    dbg_fm = DT("dbg_fm", [1, 8], F32, "ExternalOutput") if DBG_FFN else None
    tern_g = DT("tern_g", [8, 128, INNER], BF16, "Internal")
    tern_u = DT("tern_u", [8, 128, INNER], BF16, "Internal")
    tern_d = DT("tern_d", [32, 128, D], BF16, "Internal")
    v_scr = DT("v_scr", [T, NH * 65], BF16, "Internal")

    for s in ("c_vector", "c_scalar", "c_gpsimd", "c_tensor", "d_sync", "d_gpsimd"):
        p.sems[s] = ctx.enter_context(nc.semaphore(s))

    S_ = lambda n, sh, dt: ctx.enter_context(nc.sbuf_tensor(n, sh, dt))
    x_res = S_("x_res", [128, NT * D], F32)
    ident = S_("ident_t", [128, 128], BF16)
    triT = S_("triT_t", [128, 128], BF16)
    qs_t = S_("qs_t", [128, NPAIR * 128], BF16)
    kn_t = S_("kn_t", [128, NPAIR * 128], BF16)
    kp_t = S_("kp_t", [128, NPAIR * 128], BF16)
    ones_f = S_("ones_f", [128, 128], F32)
    ones_b = S_("ones_b", [128, 128], BF16)
    gs_row = S_("gs_row", [1, 1024], F32)
    gb_colb = S_("gb_colb", [128, 8], BF16)
    state_f = S_("state_f", [128, NPAIR * HD], F32)
    state_b = S_("state_b", [128, NPAIR * HD], BF16)
    cols = S_("colsbank", [128, 64], F32)
    magic_col = S_("magic_col", [128, 1], F32)
    eps_col = S_("eps_col", [128, 1], F32)
    mw_cols = S_("mw_cols", [128, 4], F32)
    PS = lambda n, sh, dt=F32: ctx.enter_context(nc.psum_tensor(n, sh, dt))
    pS4 = PS("pS4", [128, 2048])
    pA = PS("pA", [128, 512]); pB = PS("pB", [128, 512])
    pT0 = PS("pT0", [128, 1024], BF16)
    pT1 = PS("pT1", [128, 1024], BF16)
    pT = [pT0, pT1]
    pS = [pS4[:, i * 512:(i + 1) * 512] for i in range(4)]

    tok_x = p.dma("sync", x_res[:].rearrange("q (tt d) -> q tt d", d=D),
                  x_in.rearrange("(tt q) d -> q tt d", q=128))
    for dst, src in ((ident, ident_in), (triT, triT_in)):
        p.dma("sync", dst[:], src)
    for pr in range(NPAIR):
        p.dma("sync", qs_t[:, pr * 128:pr * 128 + 128], qs_in[pr])
        p.dma("sync", kn_t[:, pr * 128:pr * 128 + 128], kn_in[pr])
        p.dma("sync", kp_t[:, pr * 128:pr * 128 + 128], kp_in[pr])
    p.dma("sync", gs_row[0:1, :], gns.rearrange("(o d) -> o d", o=1))
    p.dma("gpsimd", gb_colb[:], gnb.rearrange("(c q) -> q c", q=128), allow_slow_non_contiguous=True)
    tok_c = p.dnow("sync")
    p.wait("vector", tok_c)
    p.wait("vector", p.dnow("gpsimd"))
    p.op("vector", lambda v: v.memset(magic_col[:], MAGIC))
    p.op("vector", lambda v: v.memset(eps_col[:], EPS))
    p.op("vector", lambda v: v.memset(cols[:, 63:64], 0.0))
    p.op("vector", lambda v: v.tensor_add(out=cols[:, 63:64], in0=cols[:, 63:64], in1=magic_col[:]))
    p.op("vector", lambda v: v.memset(ones_f[:], 1.0))
    p.op("vector", lambda v: v.memset(ones_b[:], 1.0))
    p.op("vector", lambda v: v.memset(state_f[:], 0.0))
    tok_init = p.op("vector", lambda v: v.memset(state_b[:], 0.0))

    colptr = [0]

    def col(n=1):
        i = colptr[0]
        colptr[0] = (i + n) % 56
        if colptr[0] < i:
            i, colptr[0] = 0, n
        return cols[:, i:i + n] if n > 1 else cols[:, i:i + 1]

    tp_state = {"last": [None, None], "j": 0, "lastA": None}

    def pA_done(tok):
        tp_state["lastA"] = tok

    def pe_transpose(src, dst, tok_src_v=None, f32=False, evac=True):
        pr_, cr_ = src.shape[0], src.shape[1]
        if f32:
            bank = pA[0:cr_, 0:pr_]
            if tp_state["lastA"] is not None:
                p.wait("tensor", tp_state["lastA"])
            idn = ident_f
        else:
            j = tp_state["j"] % 2
            tp_state["j"] += 1
            bank = pT[j][0:cr_, 0:pr_]
            if tp_state["last"][j] is not None:
                p.wait("tensor", tp_state["last"][j])
            idn = ident
        if tok_src_v is not None:
            p.wait("tensor", tok_src_v)
        tpe = p.pe(lambda t, src=src, bank=bank, idn=idn: t.transpose(
            out=bank, in_=src, identity=idn[:]), inc=True)
        if not evac:
            return tpe
        p.wait("vector", tpe)
        tok = p.op("vector", lambda v, dst=dst, bank=bank: v.tensor_copy(out=dst, in_=bank))
        if f32:
            tp_state["lastA"] = tok
        else:
            tp_state["last"][j] = tok
        return tok

    ident_f = S_("ident_f", [128, 128], F32)
    p.wait("vector", tok_c)
    p.op("vector", lambda v: v.tensor_copy(out=ident_f[:], in_=ident[:]))

    def x_tile(tt):
        return x_res[:, tt * D:(tt + 1) * D]

    def ln_stats(xt, want_quant, scr1024):
        c_sumsq = col(); c_sum = col(); c_mean = col(); c_var = col()
        c_rstd = col(); c_tmp = col()
        tok_sq = p.op("scalar", lambda s: s.activation(out=scr1024, in_=xt, func=AF.Square, accum_out=c_sumsq))
        tok_sq = p.op("scalar", lambda s: s.activation(out=c_tmp, in_=c_sumsq, func=AF.Copy))
        p.op("vector", lambda v: v.reduce_sum(out=c_sum, in_=xt, axis=AX.X))
        res = {}
        if want_quant:
            c_max = col(); c_negmin = col(); c_amax = col(); c_s = col(); c_rcps = col()
            p.op("vector", lambda v: v.reduce_max(out=c_max, in_=xt, axis=AX.X))
            p.wait("vector", tok_sq)
            p.op("vector", lambda v: v.tensor_scalar_mul(out=scr1024, in0=xt, scalar1=-1.0))
            p.op("vector", lambda v: v.reduce_max(out=c_negmin, in_=scr1024, axis=AX.X))
        p.wait("vector", tok_sq)
        p.op("vector", lambda v: v.tensor_scalar_mul(out=c_mean, in0=c_sum, scalar1=1.0 / D))
        p.op("vector", lambda v: v.tensor_mul(out=c_tmp, in0=c_mean, in1=c_mean))
        p.op("vector", lambda v: v.tensor_scalar_mul(out=c_var, in0=c_sumsq, scalar1=1.0 / D))
        p.op("vector", lambda v: v.tensor_sub(out=c_var, in0=c_var, in1=c_tmp))
        p.op("vector", lambda v: v.tensor_scalar_add(out=c_var, in0=c_var, scalar1=EPS))
        tv = p.op("vector", lambda v: v.tensor_copy(out=c_tmp, in_=c_var))
        p.wait("scalar", tv)
        p.op("scalar", lambda s: s.activation(out=c_rstd, in_=c_var, func=AF.Sqrt))
        ts_ = p.op("scalar", lambda s: s.activation(out=c_tmp, in_=c_rstd, func=AF.Copy))
        p.wait("vector", ts_)
        p.op("vector", lambda v: v.reciprocal(out=c_rstd, in_=c_rstd))
        c_scale = col(); c_bias = col()
        if not want_quant:
            p.op("vector", lambda v: v.tensor_copy(out=c_scale, in_=c_rstd))
        else:
            p.op("vector", lambda v: v.tensor_sub(out=c_tmp, in0=c_max, in1=c_mean))
            p.op("vector", lambda v: v.tensor_add(out=c_amax, in0=c_mean, in1=c_negmin))
            p.op("vector", lambda v: v.tensor_max(out=c_amax, in0=c_amax, in1=c_tmp))
            p.op("vector", lambda v: v.tensor_mul(out=c_amax, in0=c_amax, in1=c_rstd))
            p.op("vector", lambda v: v.tensor_scalar_max(out=c_amax, in0=c_amax, scalar1=1e-5))
            p.op("vector", lambda v: v.reciprocal(out=c_s, in_=c_amax))
            p.op("vector", lambda v: v.tensor_scalar_mul(out=c_s, in0=c_s, scalar1=127.0))
            p.op("vector", lambda v: v.tensor_scalar_mul(out=c_rcps, in0=c_amax, scalar1=1.0 / 127.0))
            p.op("vector", lambda v: v.tensor_mul(out=c_scale, in0=c_rstd, in1=c_s))
            res["rcp_s"] = c_rcps
        p.op("vector", lambda v: v.tensor_mul(out=c_bias, in0=c_mean, in1=c_scale))
        p.op("vector", lambda v: v.tensor_scalar_mul(out=c_bias, in0=c_bias, scalar1=-1.0))
        tok = p.op("vector", lambda v: v.tensor_add(out=c_tmp, in0=c_scale, in1=c_bias))
        if want_quant:
            tok = p.op("vector", lambda v: v.tensor_add(out=c_tmp, in0=c_tmp, in1=res["rcp_s"]))
        res.update(scale=c_scale, bias=c_bias, tok=tok)
        return res

    def stage_attn_a():
        _attn_a()

    def stage_ffn(first):
        _ffn(first)

    def stage_attn_b():
        _attn_b()

    def _attn_a():
        g = _gammas()
        sctx = ExitStack()
        A = lambda n, sh, dt: sctx.enter_context(nc.sbuf_tensor(n, sh, dt))
        y1fm = A("y1fm", [128, 8 * T], BF16)
        obuf = A("obuf", [128, NT * D], BF16)
        w1s = A("w1s", [128, 8 * D], BF16)
        w2s = A("w2s", [128, 8 * D], BF16)
        wbd = A("wbd", [128, 3 * NPAIR * 128], BF16)
        gbw2_b = A("gbw2_b", [128, D], F32)
        gsb = A("gsb", [128, D], F32)
        scr = A("scr_a", [128, 1536], F32)
        sb = A("sb_a", [128, 1792], BF16)
        w1_c = w1_a.rearrange("(c q) o -> c q o", q=128)
        w2_c = w2_a.rearrange("(c q) o -> c q o", q=128)
        for c in range(8):
            p.dma("gpsimd", w1s[:, c * D:(c + 1) * D], w1_c[c])
            p.dma("gpsimd", w2s[:, c * D:(c + 1) * D], w2_c[c])
        tok_w = p.dnow("gpsimd")
        p.wait("tensor", tok_w)
        tokmm = None
        for o in range(2):
            for c in range(8):
                tokmm = p.pe(lambda t, c=c, o=o: t.matmul(
                    out=pS[o][0:1, :], lhsT=gb_colb[:, c:c + 1],
                    rhs=w2s[:, c * D + o * 512:c * D + o * 512 + 512],
                    start=(c == 0), stop=(c == 7)), inc=(c == 7))
        p.wait("vector", tokmm)
        p.op("vector", lambda v: v.tensor_copy(out=scr[0:1, 0:512], in_=pS[0][0:1, :]))
        p.op("vector", lambda v: v.tensor_copy(out=scr[0:1, 512:1024], in_=pS[1][0:1, :]))
        tok = p.op("vector", lambda v: v.tensor_copy(out=sb[0:1, 0:1024], in_=scr[0:1, 0:1024]))
        p.wait("tensor", tok)
        for o in range(2):
            tokmm = p.pe(lambda t, o=o: t.matmul(
                out=pS[2 + o][:], lhsT=ones_b[0:1, :], rhs=sb[0:1, o * 512:o * 512 + 512],
                start=True, stop=True), inc=True)
        for o in range(2):
            tokg = p.pe(lambda t, o=o: t.matmul(
                out=pS[o][:], lhsT=ones_f[0:1, :], rhs=gs_row[0:1, o * 512:o * 512 + 512],
                start=True, stop=True), inc=True)
        p.wait("vector", tokg)
        p.op("vector", lambda v: v.tensor_copy(out=gbw2_b[:, 0:512], in_=pS[2][:]))
        p.op("vector", lambda v: v.tensor_copy(out=gbw2_b[:, 512:1024], in_=pS[3][:]))
        p.op("vector", lambda v: v.tensor_copy(out=gsb[:, 0:512], in_=pS[0][:]))
        tok_gb = p.op("vector", lambda v: v.tensor_copy(out=gsb[:, 512:1024], in_=pS[1][:]))
        tok_ms = p.op("vector", lambda v: v.memset(wbd[:], 0.0))
        p.wait("gpsimd", tok_ms)
        for wi, wsrc in enumerate((wq_a, wk_a, wv_a)):
            for h in range(NH):
                pr, half = h // 2, h % 2
                dst = wbd[64 * half:64 * half + 64,
                          wi * 1024 + pr * 128 + 64 * half: wi * 1024 + pr * 128 + 64 * half + 64]
                p.dma("gpsimd", dst, wsrc[h])
        tok_bd = p.dnow("gpsimd")
        if ATTN_A_PARTS == "prep":
            p.barrier(); sctx.close(); return

        p.wait("tensor", tok_gb)
        for tt in range(NT):
            colptr[0] = 0
            st = ln_stats(x_tile(tt), False, scr[:, 0:1024])
            p.wait("scalar", st["tok"])
            ty = p.op("scalar", lambda s, tt=tt, st=st: s.activation(
                out=sb[:, 0:1024], in_=x_tile(tt), func=AF.Identity,
                scale=st["scale"], bias=st["bias"]))
            for i in range(8):
                pe_transpose(sb[:, i * 128:(i + 1) * 128],
                             y1fm[:, i * T + tt * 128: i * T + tt * 128 + 128],
                             tok_src_v=ty)
            p.wait("scalar", p.now("tensor"))
        tok_y1 = p.now("vector")
        if ATTN_A_PARTS == "ln":
            p.barrier(); sctx.close(); return

        p.wait("tensor", tok_y1)
        evac_tok = [None, None]
        for tt in range(NT):
            for o in range(2):
                if evac_tok[o] is not None:
                    p.wait("tensor", evac_tok[o])
                for i in range(8):
                    mt = p.pe(lambda t, tt=tt, o=o, i=i: t.matmul(
                        out=pS[o][:],
                        lhsT=y1fm[:, i * T + tt * 128: i * T + tt * 128 + 128],
                        rhs=w1s[:, i * D + o * 512: i * D + o * 512 + 512],
                        start=(i == 0), stop=(i == 7)), inc=(i == 7))
                p.wait("scalar", mt)
                evac_tok[o] = p.op("scalar", lambda s, tt=tt, o=o: s.activation(
                    out=obuf[:, tt * D + o * 512: tt * D + o * 512 + 512],
                    in_=pS[o][:], func=AF.Silu))
        p.wait("tensor", evac_tok[0]); p.wait("tensor", evac_tok[1])
        p.wait("tensor", tok_bd)
        if ATTN_A_PARTS == "gate":
            p.barrier(); sctx.close(); return

        qsb = sb[:, 0:128]; knb = sb[:, 128:256]
        vtm = sb[:, 256:384]; kpb = sb[:, 384:512]
        smb = [sb[:, 512:640], sb[:, 640:768]]
        nb = scr[:, 0:1024]
        nbb = sb[:, 768:768 + 1024]
        gnst = scr[:, 1024:1024 + 24]
        banktok = {}
        state_cp = {}
        nblocks = 1 if ATTN_A_PARTS.startswith("blocks1") else NT
        for b in range(nblocks):
            for pr in range(NPAIR):
                ysl = y1fm[:, pr * T + b * 128: pr * T + b * 128 + 128]
                if banktok.get(0) is not None:
                    p.wait("tensor", banktok[0])
                p.pe(lambda t, pr=pr, ysl=ysl: t.matmul(
                    out=pS4[:, 0:128], lhsT=wbd[:, 0 * 1024 + pr * 128: 0 * 1024 + pr * 128 + 128],
                    rhs=ysl, start=True, stop=True))
                tokp2 = p.pe(lambda t, pr=pr, ysl=ysl: t.matmul(
                    out=pS4[:, 128:256], lhsT=wbd[:, 1 * 1024 + pr * 128: 1 * 1024 + pr * 128 + 128],
                    rhs=ysl, start=True, stop=True), inc=True)
                p.pe(lambda t, pr=pr, ysl=ysl: t.matmul(
                    out=pS4[:, 512:640], lhsT=ysl,
                    rhs=wbd[:, 2 * 1024 + pr * 128: 2 * 1024 + pr * 128 + 128],
                    start=True, stop=True))
                tokp = p.pe(lambda t, pr=pr, ysl=ysl: t.matmul(
                    out=pS4[:, 640:768], lhsT=ysl,
                    rhs=wbd[:, 1 * 1024 + pr * 128: 1 * 1024 + pr * 128 + 128],
                    start=True, stop=True), inc=True)
                p.wait("vector", tokp2)
                p.op("vector", lambda v, pr=pr: v.tensor_mul(
                    out=qsb, in0=pS4[:, 0:128], in1=qs_t[:, pr * 128:pr * 128 + 128]))
                tokkn = p.op("vector", lambda v, pr=pr: v.tensor_mul(
                    out=knb, in0=pS4[:, 128:256], in1=kn_t[:, pr * 128:pr * 128 + 128]))
                p.wait("vector", tokp)
                p.op("vector", lambda v: v.tensor_copy(out=vtm, in_=pS4[:, 512:640]))
                tokev = p.op("vector", lambda v, pr=pr: v.tensor_mul(
                    out=kpb, in0=pS4[:, 640:768], in1=kp_t[:, pr * 128:pr * 128 + 128]))
                banktok[0] = tokev
                if ATTN_A_PARTS == "blocks1a":
                    continue
                if banktok.get(2) is not None:
                    p.wait("tensor", banktok[2])
                p.wait("tensor", tokkn)
                for half in range(2):
                    p.pe(lambda t, half=half: t.matmul(
                        out=pS4[:, 1024 + 512 * half: 1024 + 512 * half + 128],
                        lhsT=knb[64 * half:64 * half + 64, :],
                        rhs=qsb[64 * half:64 * half + 64, :], start=True, stop=True),
                        inc=(half == 1))
                toks = p.now("tensor")
                p.wait("vector", toks)
                tokmh = []
                for half in range(2):
                    tokmh.append(p.op("vector", lambda v, half=half: v.tensor_mul(
                        out=smb[half], in0=pS4[:, 1024 + 512 * half:1024 + 512 * half + 128],
                        in1=triT[:])))
                tokm = tokmh[1]
                banktok[2] = tokm
                if ATTN_A_PARTS == "blocks1b":
                    continue
                if banktok.get(3) is not None:
                    p.wait("tensor", banktok[3])
                if state_cp.get(pr) is not None:
                    p.wait("tensor", state_cp[pr])
                p.wait("tensor", tokev)
                rslc = [slice(256, 320), slice(768, 832)]
                rettok = []
                for half in range(2):
                    p.wait("tensor", tokmh[half])
                    p.pe(lambda t, half=half: t.matmul(
                        out=pS4[:, rslc[half]],
                        lhsT=smb[half], rhs=vtm[:, 64 * half:64 * half + 64],
                        start=True, stop=False))
                    rettok.append(p.pe(lambda t, half=half, pr=pr: t.matmul(
                        out=pS4[:, rslc[half]],
                        lhsT=qsb[64 * half:64 * half + 64, :],
                        rhs=state_b[64 * half:64 * half + 64, pr * 64:pr * 64 + 64],
                        start=False, stop=True), inc=True))
                tokr = rettok[1]
                stb = pB if (pr % 2 == 0) else pA
                if banktok.get(("st", pr % 2)) is not None:
                    p.wait("tensor", banktok[("st", pr % 2)])
                p.pe(lambda t, stb=stb: t.matmul(out=stb[:, 0:64], lhsT=kpb, rhs=vtm[:, 0:64],
                                        start=True, stop=True))
                tokst = p.pe(lambda t, stb=stb: t.matmul(out=stb[:, 64:128], lhsT=kpb, rhs=vtm[:, 64:128],
                                                start=True, stop=True), inc=True)
                p.wait("vector", tokst)
                for half in range(2):
                    gp = float(g[2 * pr + half] ** 128)
                    sl = state_f[64 * half:64 * half + 64, pr * 64:pr * 64 + 64]
                    psl = stb[64 * half:64 * half + 64, 64 * half:64 * half + 64]
                    p.op("vector", lambda v, sl=sl, gp=gp: v.tensor_scalar_mul(out=sl, in0=sl, scalar1=gp))
                    p.op("vector", lambda v, sl=sl, psl=psl: v.tensor_add(out=sl, in0=sl, in1=psl))
                tokcp = p.op("vector", lambda v, pr=pr: v.tensor_copy(
                    out=state_b[:, pr * 64:pr * 64 + 64], in_=state_f[:, pr * 64:pr * 64 + 64]))
                state_cp[pr] = tokcp
                banktok[("st", pr % 2)] = tokcp
                if ATTN_A_PARTS == "blocks1c":
                    continue
                sqd = scr[:, 1112:1176]
                rtf = [scr[:, 1048:1112], scr[:, 1176:1240]]
                colptr[0] = 0
                c_sum = col(); col()
                c_ss = col(); col()
                mean2 = col(); col()
                va2 = col(); col()
                msq2 = col(); col()
                nmean2 = col(); col()
                var2 = col(); col()
                lnv2 = col(); col()
                rstd2 = col(); col()
                nbi2 = col(); col()
                dump2 = col(); col()
                w2c = lambda c: c.broadcast_free(2) if False else None
                def c2(c):
                    return cols[:, c.offs:c.offs + 2] if hasattr(c, 'offs') else c
                tevac = [None, None]
                for half in range(2):
                    p.wait("scalar", rettok[half])
                    tevac[half] = p.op("scalar", lambda s, half=half: s.activation(
                        out=rtf[half], in_=pS4[:, rslc[half]], func=AF.Copy,
                        accum_out=cols[:, 0 + half:1 + half]))
                for half in range(2):
                    p.op("scalar", lambda s, half=half: s.activation(
                        out=sqd, in_=rtf[half], func=AF.Square,
                        accum_out=cols[:, 2 + half:3 + half]))
                CS = lambda a, b: cols[:, a:b]
                p.op("scalar", lambda s: s.activation(
                    out=CS(4, 6), in_=CS(0, 2), func=AF.Copy, scale=1.0 / HD))
                p.op("scalar", lambda s: s.activation(
                    out=CS(6, 8), in_=CS(2, 4), func=AF.Copy, scale=1.0 / HD))
                p.op("scalar", lambda s: s.activation(
                    out=CS(8, 10), in_=CS(4, 6), func=AF.Square))
                p.op("scalar", lambda s: s.activation(
                    out=CS(10, 12), in_=CS(4, 6), func=AF.Copy, scale=-1.0))
                p.op("scalar", lambda s: s.activation(
                    out=CS(24, 26), in_=CS(6, 8), func=AF.Copy))
                for half in range(2):
                    p.op("scalar", lambda s, half=half: s.activation(
                        out=cols[:, 12 + half:13 + half], in_=cols[:, 8 + half:9 + half],
                        func=AF.Identity, scale=-1.0, bias=cols[:, 6 + half:7 + half]))
                p.op("scalar", lambda s: s.activation(
                    out=CS(14, 16), in_=CS(12, 14), func=AF.Ln, bias=eps_col[:]))
                p.op("scalar", lambda s: s.activation(
                    out=CS(16, 18), in_=CS(14, 16), func=AF.Exp, scale=-0.5))
                p.op("scalar", lambda s: s.activation(
                    out=CS(24, 26), in_=CS(10, 12), func=AF.Copy))
                for half in range(2):
                    p.op("scalar", lambda s, half=half: s.activation(
                        out=cols[:, 18 + half:19 + half], in_=cols[:, 16 + half:17 + half],
                        func=AF.Identity, scale=cols[:, 10 + half:11 + half]))
                p.op("scalar", lambda s: s.activation(
                    out=CS(24, 26), in_=CS(18, 20), func=AF.Copy))
                tn = None
                for half in range(2):
                    tn = p.op("scalar", lambda s, half=half, pr=pr: s.activation(
                        out=nb[:, pr * 128 + 64 * half: 64 * half + pr * 128 + 64],
                        in_=rtf[half],
                        func=AF.Identity, scale=cols[:, 16 + half:17 + half],
                        bias=cols[:, 18 + half:19 + half]))
                banktok[3] = tevac[1]
                banktok[0] = tevac[1]
            if ATTN_A_PARTS in ("blocks1a", "blocks1b", "blocks1c"):
                continue
            p.wait("vector", tn)
            p.op("vector", lambda v: v.tensor_mul(out=nb[:, 0:1024], in0=nb[:, 0:1024], in1=gsb[:, 0:1024]))
            p.op("vector", lambda v: v.tensor_copy(out=nbb, in_=nb[:, 0:1024]))
            toknb = p.op("vector", lambda v, b=b: v.tensor_add(
                out=obuf[:, b * D:(b + 1) * D], in0=obuf[:, b * D:(b + 1) * D], in1=nbb))
            p.wait("scalar", toknb)

        if DBG_OBUF:
            p.barrier()
            tokdo = p.dma("sync", dbg_obuf, obuf[:])
            p.wait("sync", tokdo)
        if ATTN_A_PARTS.startswith("blocks"):
            p.barrier(); sctx.close(); return

        tok_blocks = p.now("vector")
        p.wait("tensor", tok_blocks)
        o2fm = sb[:, 0:1024]
        evac_tok = [None, None]
        for tt in range(NT):
            tok_tp = None
            for i in range(8):
                tok_tp = pe_transpose(obuf[:, tt * D + i * 128: tt * D + i * 128 + 128],
                                      o2fm[:, i * 128:(i + 1) * 128])
            p.wait("tensor", tok_tp)
            for o in range(2):
                if evac_tok[o] is not None:
                    p.wait("tensor", evac_tok[o])
                for i in range(8):
                    mt = p.pe(lambda t, o=o, i=i: t.matmul(
                        out=pS[o][:], lhsT=o2fm[:, i * 128:(i + 1) * 128],
                        rhs=w2s[:, i * D + o * 512: i * D + o * 512 + 512],
                        start=(i == 0), stop=(i == 7)), inc=(i == 7))
                p.wait("vector", mt)
                p.op("vector", lambda v, o=o: v.tensor_add(
                    out=scr[:, 1024:1536], in0=pS[o][:], in1=gbw2_b[:, o * 512:o * 512 + 512]))
                xt = x_res[:, tt * D + o * 512: tt * D + o * 512 + 512]
                evac_tok[o] = p.op("vector", lambda v, xt=xt: v.tensor_add(
                    out=xt, in0=xt, in1=scr[:, 1024:1536]))
        p.barrier()
        sctx.close()

    def _ffn(first):
        _ffn_impl(first)

    def _ffn_impl(first):
        sctx = ExitStack()
        A = lambda n, sh, dt: sctx.enter_context(nc.sbuf_tensor(n, sh, dt))
        sfx = "f1" if first else "f2"
        if first:
            _ternarize(sctx, A, sfx)
        _ffn_compute(sctx, A, sfx)
        p.barrier()
        sctx.close()

    def _ternarize(sctx_unused, A_unused, sfx):
        tctx = ExitStack()
        A = lambda n, sh, dt: tctx.enter_context(nc.sbuf_tensor(n, sh, dt))
        wbuf = [A(f"tw{i}_{sfx}", [128, INNER], F32) for i in range(2)]
        dump = A("tdump_" + sfx, [128, INNER], BF16)
        fscr = A("tfscr_" + sfx, [128, INNER], F32)
        stg = A("tstg_" + sfx, [128, 32 * 128], BF16)
        awsum = col()
        c_t = col()
        plans = [(wg_f, 32, D, tern_g, 0), (wu_f, 32, D, tern_u, 1), (wd_f, 8, INNER, tern_d, 2)]
        for (wsrc, nchunk, width, tdst, mwi) in plans:
            colptr[0] = 8
            w_c = wsrc.rearrange("(c q) i -> c q i", q=128)
            p.op("vector", lambda v: v.memset(awsum, 0.0))
            last_use = [None, None]
            last_add = None
            for c in range(nchunk):
                buf = wbuf[c % 2]
                if last_use[c % 2] is not None:
                    p.wait("sync", last_use[c % 2])
                tokd = p.dma("sync", buf[:, 0:width], w_c[c])
                p.wait("scalar", tokd)
                p.wait("scalar", last_add)
                p.op("scalar", lambda s, buf=buf, width=width: s.activation(
                    out=dump[:, 0:width], in_=buf[:, 0:width], func=AF.Abs, accum_out=c_t))
                tth = p.op("scalar", lambda s: s.activation(out=cols[:, 62:63], in_=c_t, func=AF.Copy))
                p.wait("vector", tth)
                last_add = p.op("vector", lambda v: v.tensor_add(out=awsum, in0=awsum, in1=c_t))
                last_use[c % 2] = tth
            tok = p.op("vector", lambda v: v.tensor_copy(out=cols[:, 62:63], in_=awsum))
            p.wait("tensor", tok)
            tokmm = p.pe(lambda t: t.matmul(out=pA[0:1, 0:1], lhsT=awsum, rhs=ones_f[:, 0:1],
                                            start=True, stop=True), inc=True)
            p.wait("vector", tokmm)
            nelem = float(nchunk * 128 * width)
            c_tot = col()
            p.op("vector", lambda v, c_tot=c_tot: v.tensor_copy(out=c_tot[0:1, :], in_=pA[0:1, 0:1]))
            tokt = p.op("vector", lambda v, c_tot=c_tot: v.tensor_copy(out=cols[0:1, 62:63], in_=c_tot[0:1, :]))
            pA_done(tokt)
            p.wait("tensor", tokt)
            tokmm = p.pe(lambda t, c_tot=c_tot: t.matmul(out=pA[:, 0:1], lhsT=ones_f[0:1, :],
                                            rhs=c_tot[0:1, :], start=True, stop=True), inc=True)
            p.wait("vector", tokmm)
            mc_col = col(); negmc_col = col(); sw_col = col()
            p.op("vector", lambda v, nelem=nelem: v.tensor_scalar(
                out=mc_col, in0=pA[:, 0:1], scalar1=1.0 / nelem, scalar2=1e-5,
                op0=ALU.mult, op1=ALU.max))
            p.op("vector", lambda v, mwi=mwi: v.tensor_copy(out=mw_cols[:, mwi:mwi + 1], in_=mc_col))
            p.op("vector", lambda v: v.tensor_scalar_mul(out=negmc_col, in0=mc_col, scalar1=-1.0))
            p.op("vector", lambda v: v.reciprocal(out=sw_col, in_=mc_col))
            p.op("vector", lambda v: v.tensor_add(out=cols[:, 62:63], in0=sw_col, in1=negmc_col))
            tok = p.op("vector", lambda v: v.tensor_add(out=cols[:, 62:63], in0=cols[:, 62:63], in1=mc_col))
            pA_done(tok)
            last_use = [None, None]
            last_stg = None
            ntile = width // 128
            for c in range(nchunk):
                buf = wbuf[c % 2]
                if last_use[c % 2] is not None:
                    p.wait("sync", last_use[c % 2])
                tokd = p.dma("sync", buf[:, 0:width], w_c[c])
                p.wait("vector", tokd)
                p.wait("vector", tok)
                p.op("vector", lambda v, buf=buf, width=width, mc_col=mc_col: v.tensor_scalar_min(
                    out=buf[:, 0:width], in0=buf[:, 0:width], scalar1=mc_col))
                tokg = p.op("vector", lambda v, buf=buf, width=width, negmc_col=negmc_col: v.tensor_scalar_max(
                    out=buf[:, 0:width], in0=buf[:, 0:width], scalar1=negmc_col))
                p.wait("scalar", tokg)
                p.wait("scalar", tok)
                toka = p.op("scalar", lambda s, buf=buf, width=width, sw_col=sw_col: s.activation(
                    out=fscr[:, 0:width], in_=buf[:, 0:width], func=AF.Identity,
                    scale=sw_col, bias=magic_col[:]))
                p.wait("vector", toka)
                p.wait("vector", p.now("tensor"))
                if last_stg is not None:
                    p.wait("vector", last_stg)
                tokt = p.op("vector", lambda v, width=width: v.tensor_scalar_add(
                    out=dump[:, 0:width], in0=fscr[:, 0:width], scalar1=-MAGIC))
                for j in range(ntile):
                    pe_transpose(dump[:, j * 128:(j + 1) * 128], stg[:, j * 128:(j + 1) * 128],
                                 tok_src_v=tokt)
                last_use[c % 2] = toka
                tokv = p.now("vector")
                p.wait("sync", tokv)
                dst = tdst[:, :, c * 128:(c + 1) * 128].rearrange("i q o -> q i o")
                tokso = p.dma("sync", dst, stg[:, 0:ntile * 128].rearrange("q (i o) -> q i o", o=128))
                last_stg = tokso
        p.barrier()
        tctx.close()

    def _ffn_compute(sctx, A, sfx):
        arena = A("harena_" + sfx, [128, 17408], F32)
        hfm = arena[:, 0:16384]
        ab16 = arena[:].bitcast(BF16)
        dfm = arena[:, 9216:13312]
        hq0 = A("hq0_" + sfx, [128, 512], BF16)
        yqf = A("yqf_" + sfx, [128, 8 * 512], BF16)
        yqtm = ab16[:, 0:1024]
        wstr = [A(f"wstr{i}_{sfx}", [128, 2048], BF16) for i in range(2)]
        wdstr = [A(f"wdstr{i}_{sfx}", [128, 4096], BF16) for i in range(2)]
        scrq = arena[:, 16384:17408]
        scrA2 = [A(f"scrA{i}_{sfx}", [128, 512], F32) for i in range(2)]
        scrB2 = [A(f"scrB{i}_{sfx}", [128, 512], F32) for i in range(2)]
        scrC2 = [A(f"scrC{i}_{sfx}", [128, 512], F32) for i in range(2)]
        scrA = scrA2[0]; scrB = scrB2[0]
        hmax = A("hmax_" + sfx, [128, 512], F32)
        hmin = A("hmin_" + sfx, [128, 512], F32)
        bcg = A("bcg_" + sfx, [128, 512], F32)
        bcu = A("bcu_" + sfx, [128, 512], F32)
        bcs2 = A("bcs2_" + sfx, [128, 512], F32)
        bcd = A("bcd_" + sfx, [128, 512], F32)
        rowbA = arena[0:1, 0:1536]
        rowbS = arena[0:1, 16384:17408]
        rowjunk = cols[0:1, 0:8]

        def hq_view(oc):
            return hq0[:] if oc == 0 else ab16[:, oc * 512:(oc + 1) * 512]

        wtok = [None, None]
        wtokd = [None, None]
        tokd_arr = [None, None]
        tokdd = [None, None]

        for gi in range(4):
            if gi > 0:
                p.wait("scalar", p.now("vector"))
                p.wait("scalar", p.now("tensor"))
            colptr[0] = 0
            sts = []
            for tsub in range(4):
                tt = gi * 4 + tsub
                st = ln_stats(x_tile(tt), True, scrq[:, 0:1024])
                sts.append(st)
                p.wait("scalar", st["tok"])
                ta = p.op("scalar", lambda s, tt=tt, st=st: s.activation(
                    out=scrq[:, 0:1024], in_=x_tile(tt), func=AF.Identity,
                    scale=st["scale"], bias=st["bias"]))
                p.wait("vector", ta)
                tq = p.op("vector", lambda v: v.tensor_scalar(
                    out=yqtm, in0=scrq[:, 0:1024], scalar1=MAGIC, scalar2=-MAGIC,
                    op0=ALU.add, op1=ALU.add))
                for i in range(8):
                    pe_transpose(yqtm[:, i * 128:(i + 1) * 128],
                                 yqf[:, i * 512 + tsub * 128: i * 512 + tsub * 128 + 128],
                                 tok_src_v=tq)
                p.wait("scalar", p.now("tensor"))
                p.wait("vector", p.now("tensor"))
            tok_yq = p.now("vector")
            for tsub in range(4):
                tok_tr = pe_transpose(sts[tsub]["rcp_s"], rowbA[0:1, tsub * 128:tsub * 128 + 128],
                                      f32=True, tok_src_v=sts[tsub]["tok"])
            p.wait("vector", tok_tr)
            p.op("vector", lambda v: v.tensor_scalar(
                out=rowbA[0:1, 512:1024], in0=rowbA[0:1, 0:512],
                scalar1=mw_cols[0:1, 0:1], scalar2=None, op0=ALU.mult))
            p.op("vector", lambda v: v.tensor_scalar(
                out=rowbA[0:1, 1024:1536], in0=rowbA[0:1, 0:512],
                scalar1=mw_cols[0:1, 1:2], scalar2=None, op0=ALU.mult))
            tok = p.op("vector", lambda v: v.tensor_add(
                out=rowjunk, in0=rowbA[0:1, 512:520], in1=rowbA[0:1, 1024:1032]))
            p.wait("tensor", tok)
            tk1 = p.pe(lambda t: t.matmul(out=pB[:], lhsT=ones_f[0:1, :],
                                          rhs=rowbA[0:1, 512:1024], start=True, stop=True), inc=True)
            p.wait("vector", tk1)
            tokbc = p.op("vector", lambda v: v.tensor_copy(out=bcg[:], in_=pB[:]))
            p.wait("tensor", tokbc)
            tk1 = p.pe(lambda t: t.matmul(out=pB[:], lhsT=ones_f[0:1, :],
                                          rhs=rowbA[0:1, 1024:1536], start=True, stop=True), inc=True)
            p.wait("vector", tk1)
            tokbc = p.op("vector", lambda v: v.tensor_copy(out=bcu[:], in_=pB[:]))
            p.wait("tensor", tok_yq)
            p.wait("tensor", tokbc)
            ptok = [None, None]
            siltok = [None, None]
            htok = [None, None]
            prevh = None
            tok_hm = None

            def issue_gu_dma(oc_):
                par_ = oc_ % 2
                p.wait("sync", wtok[par_])
                gsl = tern_g[:, :, oc_ * 128:(oc_ + 1) * 128].rearrange("i q o -> q i o")
                usl = tern_u[:, :, oc_ * 128:(oc_ + 1) * 128].rearrange("i q o -> q i o")
                p.dma("sync", wstr[par_][:, 0:1024].rearrange("q (i o) -> q i o", o=128), gsl)
                tokd_arr[par_] = p.dma("sync", wstr[par_][:, 1024:2048].rearrange("q (i o) -> q i o", o=128), usl)
            issue_gu_dma(0)
            issue_gu_dma(1)

            def lagged_max(oc_, tokh_):
                p.wait("vector", tokh_)
                if oc_ == 0:
                    p.op("vector", lambda v: v.tensor_copy(out=hmax[:], in_=hfm[:, 0:512]))
                    return p.op("vector", lambda v: v.tensor_copy(out=hmin[:], in_=hfm[:, 0:512]))
                p.op("vector", lambda v, oc_=oc_: v.tensor_tensor(
                    out=hmax[:], in0=hmax[:], in1=hfm[:, oc_ * 512:(oc_ + 1) * 512], op=ALU.max))
                return p.op("vector", lambda v, oc_=oc_: v.tensor_tensor(
                    out=hmin[:], in0=hmin[:], in1=hfm[:, oc_ * 512:(oc_ + 1) * 512], op=ALU.min))

            for oc in range(32):
                par = oc % 2
                tokd = tokd_arr[par]
                p.wait("tensor", tokd)
                if ptok[par] is not None:
                    p.wait("tensor", ptok[par])
                for i in range(8):
                    p.pe(lambda t, par=par, i=i: t.matmul(
                        out=pS[2 * par], lhsT=wstr[par][:, i * 128:(i + 1) * 128],
                        rhs=yqf[:, i * 512:(i + 1) * 512], start=(i == 0), stop=(i == 7)))
                    mt = p.pe(lambda t, par=par, i=i: t.matmul(
                        out=pS[2 * par + 1], lhsT=wstr[par][:, 1024 + i * 128:1024 + i * 128 + 128],
                        rhs=yqf[:, i * 512:(i + 1) * 512], start=(i == 0), stop=(i == 7)),
                        inc=(i == 7))
                wtok[par] = mt
                if oc + 2 < 32:
                    p.wait("sync", mt)
                    issue_gu_dma(oc + 2)
                p.wait("vector", mt)
                p.wait("vector", siltok[par])
                p.op("vector", lambda v, par=par: v.tensor_mul(out=scrA2[par][:], in0=pS[2 * par], in1=bcg[:]))
                p.wait("vector", htok[par])
                tokup = p.op("vector", lambda v, par=par: v.tensor_mul(out=scrC2[par][:], in0=pS[2 * par + 1], in1=bcu[:]))
                ptok[par] = tokup
                p.wait("scalar", tokup)
                p.wait("scalar", htok[par])
                toksl = p.op("scalar", lambda s, par=par: s.activation(out=scrB2[par][:], in_=scrA2[par][:], func=AF.Silu))
                siltok[par] = toksl
                p.wait("gpsimd", toksl)
                tokh = p.op("gpsimd", lambda gq, oc=oc, par=par: gq.tensor_mul(
                    out=hfm[:, oc * 512:(oc + 1) * 512], in0=scrB2[par][:], in1=scrC2[par][:]))
                htok[par] = tokh
                if prevh is not None:
                    tok_hm = lagged_max(*prevh)
                prevh = (oc, tokh)
            tok_hm = lagged_max(*prevh)
            p.wait("vector", tok_hm)
            p.op("vector", lambda v: v.tensor_scalar_mul(out=hmin[:], in0=hmin[:], scalar1=-1.0))
            tok_hm = p.op("vector", lambda v: v.tensor_max(out=hmax[:], in0=hmax[:], in1=hmin[:]))
            p.wait("tensor", tok_hm)
            c_t = cols[:, 60:61]
            for tsub in range(4):
                tok_tr = pe_transpose(hmax[:, tsub * 128:tsub * 128 + 128],
                                      None, f32=True, evac=False)
                p.wait("vector", tok_tr)
                tokr = p.op("vector", lambda v: v.reduce_max(
                    out=cols[:, 58:59], in_=pA[0:128, 0:128], axis=AX.X))
                pA_done(tokr)
                p.op("vector", lambda v: v.tensor_scalar_max(out=cols[:, 58:59], in0=cols[:, 58:59], scalar1=1e-5))
                p.op("vector", lambda v: v.reciprocal(out=cols[:, 59:60], in_=cols[:, 58:59]))
                p.op("vector", lambda v: v.tensor_scalar_mul(out=cols[:, 59:60], in0=cols[:, 59:60], scalar1=127.0))
                p.op("vector", lambda v: v.tensor_scalar_mul(out=cols[:, 58:59], in0=cols[:, 58:59], scalar1=1.0 / 127.0))
                tok = p.op("vector", lambda v, c_t=c_t: v.tensor_add(out=c_t, in0=cols[:, 58:59], in1=cols[:, 59:60]))
                pe_transpose(cols[:, 59:60], rowbS[0:1, tsub * 128:tsub * 128 + 128],
                             f32=True, tok_src_v=tok)
                tok_t2 = pe_transpose(cols[:, 58:59], rowbS[0:1, 512 + tsub * 128:512 + tsub * 128 + 128],
                                      f32=True, tok_src_v=tok)
            p.wait("vector", tok_t2)
            tokr = p.op("vector", lambda v: v.tensor_scalar(
                out=rowbS[0:1, 512:1024], in0=rowbS[0:1, 512:1024],
                scalar1=mw_cols[0:1, 2:3], scalar2=None, op0=ALU.mult))
            tok = p.op("vector", lambda v: v.tensor_add(
                out=rowjunk, in0=rowbS[0:1, 0:8], in1=rowbS[0:1, 512:520]))
            p.wait("tensor", tok)
            tk1 = p.pe(lambda t: t.matmul(out=pB[:], lhsT=ones_f[0:1, :],
                                          rhs=rowbS[0:1, 0:512], start=True, stop=True), inc=True)
            p.wait("vector", tk1)
            tokbc = p.op("vector", lambda v: v.tensor_copy(out=bcs2[:], in_=pB[:]))
            p.wait("tensor", tokbc)
            tk1 = p.pe(lambda t: t.matmul(out=pB[:], lhsT=ones_f[0:1, :],
                                          rhs=rowbS[0:1, 512:1024], start=True, stop=True), inc=True)
            p.wait("vector", tk1)
            tokbc = p.op("vector", lambda v: v.tensor_copy(out=bcd[:], in_=pB[:]))
            toklast = None
            for oc in range(32):
                p.wait("gpsimd", tokbc)
                tokm = p.op("gpsimd", lambda gq, oc=oc: gq.tensor_mul(
                    out=scrA[:] if oc % 2 == 0 else scrB[:],
                    in0=hfm[:, oc * 512:(oc + 1) * 512], in1=bcs2[:]))
                p.wait("vector", tokm)
                toklast = p.op("vector", lambda v, oc=oc: v.tensor_scalar(
                    out=hq_view(oc), in0=scrA[:] if oc % 2 == 0 else scrB[:],
                    scalar1=MAGIC, scalar2=-MAGIC, op0=ALU.add, op1=ALU.add))
                p.wait("gpsimd", toklast)
            if DBG_FFN and gi == 0 and sfx == "f1":
                p.barrier()
                p.dma("sync", dbg_fa, arena[:])
                p.dma("sync", dbg_fy, yqf[:])
                p.dma("sync", dbg_fb[:, 0:512], bcg[:])
                p.dma("sync", dbg_fb[:, 512:1024], bcu[:])
                p.dma("sync", dbg_fb[:, 1024:1536], bcs2[:])
                p.dma("sync", dbg_fb[:, 1536:2048], bcd[:])
                p.dma("sync", dbg_fr[0:1, 0:1024], rowbS[0:1, 0:1024])
                tokdf = p.dma("sync", dbg_fm[0:1, 0:4], mw_cols[0:1, 0:4])
                p.wait("sync", tokdf)
            p.wait("tensor", toklast)
            p.wait("tensor", tokbc)
            dtok = [None, None]

            def issue_d_dma(cc_):
                p.wait("sync", wtokd[cc_ % 2])
                dsl = tern_d[:, :, cc_ * 128:(cc_ + 1) * 128].rearrange("i q o -> q i o")
                tokdd[cc_ % 2] = p.dma("sync", wdstr[cc_ % 2][:, 0:4096].rearrange("q (i o) -> q i o", o=128), dsl)
            issue_d_dma(0)
            issue_d_dma(1)
            for cc in range(8):
                par = cc % 2
                tokd = tokdd[par]
                p.wait("tensor", tokd)
                if dtok[par] is not None:
                    p.wait("tensor", dtok[par])
                for j in range(32):
                    mt = p.pe(lambda t, par=par, j=j: t.matmul(
                        out=pS[2 * par], lhsT=wdstr[par][:, j * 128:(j + 1) * 128],
                        rhs=hq_view(j), start=(j == 0), stop=(j == 31)), inc=(j == 31))
                wtokd[par] = mt
                if cc + 2 < 8:
                    p.wait("sync", mt)
                    issue_d_dma(cc + 2)
                p.wait("vector", mt)
                tokdf = p.op("vector", lambda v, cc=cc, par=par: v.tensor_mul(
                    out=dfm[:, cc * 512:(cc + 1) * 512], in0=pS[2 * par], in1=bcd[:]))
                dtok[par] = tokdf
            p.wait("tensor", dtok[0]); p.wait("tensor", dtok[1])
            for cb in range(8):
                for tsub in range(4):
                    tok_tr = pe_transpose(dfm[:, cb * 512 + tsub * 128: cb * 512 + tsub * 128 + 128],
                                          None, f32=True, evac=False)
                    p.wait("vector", tok_tr)
                    xt = x_res[:, (gi * 4 + tsub) * D + cb * 128: (gi * 4 + tsub) * D + cb * 128 + 128]
                    tokx = p.op("vector", lambda v, xt=xt: v.tensor_add(out=xt, in0=xt, in1=pA[0:128, 0:128]))
                    pA_done(tokx)
            p.wait("tensor", tokx)

    def _attn_b():
        sctx = ExitStack()
        A = lambda n, sh, dt: sctx.enter_context(nc.sbuf_tensor(n, sh, dt))
        qrope = A("qrope", [128, 8 * T], BF16)
        krope = A("krope", [128, 8 * T], BF16)
        sub1 = ExitStack()
        A1 = lambda n, sh, dt: sub1.enter_context(nc.sbuf_tensor(n, sh, dt))
        y3fm = A1("y3fm", [128, 8 * T], BF16)
        subv = ExitStack()
        Av = lambda n, sh, dt: subv.enter_context(nc.sbuf_tensor(n, sh, dt))
        wvs = Av("wvs", [128, 8 * D], BF16)
        vstage = Av("vstage", [128, NH * 65], BF16)
        wtmp = Av("wtmp_b", [128, D], BF16)
        sbq = qrope[:, 0:1024]
        scrq = qrope[:, 2048:4096].bitcast(F32)

        for tt in range(NT):
            colptr[0] = 0
            st = ln_stats(x_tile(tt), False, scrq[:, 0:1024])
            p.wait("scalar", st["tok"])
            ty = p.op("scalar", lambda s, tt=tt, st=st: s.activation(
                out=sbq[:], in_=x_tile(tt), func=AF.Identity,
                scale=st["scale"], bias=st["bias"]))
            for i in range(8):
                pe_transpose(sbq[:, i * 128:(i + 1) * 128],
                             y3fm[:, i * T + tt * 128: i * T + tt * 128 + 128],
                             tok_src_v=ty)
            p.wait("scalar", p.now("tensor"))
        tok_y3 = p.now("vector")

        Wv_c = Wv_b.rearrange("(c q) i -> c q i", q=128)
        for oc in range(8):
            tokd = p.dma("gpsimd", wtmp[:], Wv_c[oc])
            p.wait("tensor", tokd)
            for i in range(8):
                pe_transpose(wtmp[:, i * 128:(i + 1) * 128],
                             wvs[:, i * D + oc * 128: i * D + oc * 128 + 128])
            p.wait("gpsimd", p.now("tensor"))

        p.wait("tensor", tok_y3)
        ev = [None, None]
        tokvs = None
        for tt in range(NT):
            if tokvs is not None:
                p.wait("vector", tokvs)
            tkm = p.op("vector", lambda v: v.memset(
                vstage[:].rearrange("q (h c) -> q h c", c=65)[:, :, 64:65], 1.0))
            for o in range(2):
                if ev[o] is not None:
                    p.wait("tensor", ev[o])
                mt = None
                for i in range(8):
                    mt = p.pe(lambda t, tt=tt, o=o, i=i: t.matmul(
                        out=pS[o][:], lhsT=y3fm[:, i * T + tt * 128: i * T + tt * 128 + 128],
                        rhs=wvs[:, i * D + o * 512: i * D + o * 512 + 512],
                        start=(i == 0), stop=(i == 7)), inc=(i == 7))
                p.wait("vector", mt)
                dview = vstage[:].rearrange("q (h c) -> q h c", c=65)[:, o * 8:(o + 1) * 8, 0:64]
                sview = pS[o][:].rearrange("q (h c) -> q h c", c=64)
                ev[o] = p.op("vector", lambda v, dview=dview, sview=sview: v.tensor_copy(
                    out=dview, in_=sview))
            p.wait("sync", ev[1])
            tokvs = p.dma("sync", v_scr[tt * 128:(tt + 1) * 128, :], vstage[:])

        subv.close()
        subq = ExitStack()
        Aq = lambda n, sh, dt: subq.enter_context(nc.sbuf_tensor(n, sh, dt))
        wtmp = Aq("wtmpq_b", [128, D], BF16)
        wcol = Aq("wcol_b", [128, D], BF16)
        wrot = Aq("wrot_b", [128, D], BF16)
        scrA = Aq("scrA_b", [128, 512], F32)
        scrB = Aq("scrB_b", [128, 512], F32)
        ropeC = Aq("ropeC_t", [128, T], F32)
        ropeS = Aq("ropeS_t", [128, T], F32)
        p.dma("sync", ropeC[:], ropeC_in)
        tok_rt = p.dma("sync", ropeS[:], ropeS_in)
        p.wait("vector", tok_rt)
        bankt = {}
        for wsrc, dest in ((Wq_b, qrope), (Wk_b, krope)):
            W_c = wsrc.rearrange("(c q) i -> c q i", q=128)
            for oc in range(8):
                tokd = p.dma("gpsimd", wtmp[:], W_c[oc])
                p.wait("tensor", tokd)
                tok_tw = None
                for i in range(8):
                    tok_tw = pe_transpose(wtmp[:, i * 128:(i + 1) * 128],
                                          wcol[:, i * 128:(i + 1) * 128])
                p.wait("gpsimd", p.now("tensor"))
                p.wait("vector", tok_tw)
                p.op("vector", lambda v: v.tensor_scalar_mul(
                    out=wrot[:, 0:1024:2], in0=wcol[:, 1:1024:2], scalar1=-1.0))
                tokrot = p.op("vector", lambda v: v.tensor_copy(
                    out=wrot[:, 1:1024:2], in_=wcol[:, 0:1024:2]))
                p.wait("tensor", tokrot)
                for tgp in range(2):
                    for bk in range(4):
                        if bankt.get(bk) is not None:
                            p.wait("tensor", bankt[bk])
                    mt = None
                    for i in range(8):
                        for tg2 in range(2):
                            tg = 2 * tgp + tg2
                            p.pe(lambda t, i=i, tg=tg, tg2=tg2: t.matmul(
                                out=pS[tg2][:], lhsT=wcol[:, i * 128:(i + 1) * 128],
                                rhs=y3fm[:, i * T + tg * 512: i * T + tg * 512 + 512],
                                start=(i == 0), stop=(i == 7)))
                            mt = p.pe(lambda t, i=i, tg=tg, tg2=tg2: t.matmul(
                                out=pS[2 + tg2][:], lhsT=wrot[:, i * 128:(i + 1) * 128],
                                rhs=y3fm[:, i * T + tg * 512: i * T + tg * 512 + 512],
                                start=(i == 0), stop=(i == 7)),
                                inc=(i == 7 and tg2 == 1))
                    p.wait("vector", mt)
                    for tg2 in range(2):
                        tg = 2 * tgp + tg2
                        tsl = slice(tg * 512, tg * 512 + 512)
                        p.op("vector", lambda v, tg2=tg2, tsl=tsl: v.tensor_mul(
                            out=scrA[:], in0=pS[tg2][:], in1=ropeC[:, tsl]))
                        p.op("vector", lambda v, tg2=tg2, tsl=tsl: v.tensor_mul(
                            out=scrB[:], in0=pS[2 + tg2][:], in1=ropeS[:, tsl]))
                        tokev = p.op("vector", lambda v, dest=dest, oc=oc, tg=tg: v.tensor_add(
                            out=dest[:, oc * T + tg * 512: oc * T + tg * 512 + 512],
                            in0=scrA[:], in1=scrB[:]))
                        bankt[tg2] = tokev
                        bankt[2 + tg2] = tokev
        p.barrier()
        subq.close()
        sub1.close()

        sub2 = ExitStack()
        A2 = lambda n, sh, dt: sub2.enter_context(nc.sbuf_tensor(n, sh, dt))
        wos2 = A2("wos2", [64, NH * 1024], BF16)
        abuf = A2("abuf", [64, NH * 512], BF16)
        vhb = [A2("vhb0", [128, NT * 65], BF16), A2("vhb1", [128, NT * 65], BF16)]
        pring = A2("pring", [128, 2048], BF16)
        dsb = pring[:].bitcast(F32)[:, 0:512]
        wtmp2 = vhb[0][:, 0:1024]
        bcf = pring[:].bitcast(F32)[0:64, 512:1024]
        tch = cols[:, 61:62]
        Wo_c = Wo_b.rearrange("(c q) o -> c q o", q=128)
        for cc in range(8):
            tokd = p.dma("gpsimd", wtmp2[:], Wo_c[cc])
            p.wait("tensor", tokd)
            for h in range(NH):
                pe_transpose(wtmp2[:, h * 64:(h + 1) * 64],
                             wos2[0:64, h * 1024 + cc * 128: h * 1024 + cc * 128 + 128])
            p.wait("gpsimd", p.now("tensor"))
        tok_wo = p.now("vector")
        p.wait("tensor", tok_wo)

        vtok = [None, None]
        aev = None
        pvt = None
        texp = None
        pvt2 = [None, None]
        texp2 = [None, None]
        for qg in range(4):
            for h in range(NH):
                pr, half = h // 2, h % 2
                if vtok[h % 2] is not None:
                    p.wait("gpsimd", vtok[h % 2])
                vsl = v_scr[:, 65 * h: 65 * h + 65].rearrange("(kt q) c -> q kt c", q=128)
                tokv = p.dma("gpsimd", vhb[h % 2][:].rearrange("q (kt c) -> q kt c", c=65), vsl)
                p.wait("tensor", tokv)
                for kt2 in range(8):
                    ph = kt2 % 2
                    smt = None
                    if texp2[ph] is not None:
                        p.wait("tensor", texp2[ph])
                    for k2 in range(2):
                        kt = kt2 * 2 + k2
                        smt = p.pe(lambda t, pr=pr, half=half, kt=kt, k2=k2, ph=ph, qg=qg: t.matmul(
                            out=pS4[:, ph * 1024 + k2 * 512: ph * 1024 + k2 * 512 + 512],
                            lhsT=krope[64 * half:64 * half + 64, pr * T + kt * 128: pr * T + kt * 128 + 128],
                            rhs=qrope[64 * half:64 * half + 64, pr * T + qg * 512: pr * T + qg * 512 + 512],
                            start=True, stop=True), inc=(k2 == 1))
                    p.wait("scalar", smt)
                    if pvt2[ph] is not None:
                        p.wait("scalar", pvt2[ph])
                    p.wait("scalar", aev)
                    texp2[ph] = p.op("scalar", lambda s, ph=ph: s.activation(
                        out=pring[:, ph * 1024: ph * 1024 + 1024],
                        in_=pS4[:, ph * 1024: ph * 1024 + 1024],
                        func=AF.Exp, scale=float(HD) ** -0.5))
                    texp = texp2[ph]
                    p.wait("tensor", texp)
                    for k2 in range(2):
                        kt = kt2 * 2 + k2
                        pvt2[ph] = p.pe(lambda t, h=h, kt=kt, k2=k2, ph=ph: t.matmul(
                            out=pB[0:65, :], lhsT=vhb[h % 2][:, kt * 65: kt * 65 + 65],
                            rhs=pring[:, ph * 1024 + k2 * 512: ph * 1024 + k2 * 512 + 512],
                            start=(kt == 0), stop=(kt == 15)), inc=(k2 == 1))
                    pvt = pvt2[ph]
                vtok[h % 2] = pvt
                p.wait("vector", pvt)
                p.op("vector", lambda v: v.reciprocal(out=dsb[64:65, :], in_=pB[64:65, :]))
                tokt = p.op("vector", lambda v: v.reduce_sum(out=tch[64:65, :], in_=dsb[64:65, :], axis=AX.X))
                p.wait("tensor", tokt)
                if tp_state["lastA"] is not None:
                    p.wait("tensor", tp_state["lastA"])
                tokbc = p.pe(lambda t: t.matmul(out=pA[0:64, :], lhsT=ones_f[64:65, 0:64],
                                                rhs=dsb[64:65, :], start=True, stop=True), inc=True)
                p.wait("vector", tokbc)
                tokb = p.op("vector", lambda v: v.tensor_copy(out=bcf[0:64, :], in_=pA[0:64, :]))
                pA_done(tokb)
                aev = p.op("vector", lambda v, h=h: v.tensor_mul(
                    out=abuf[0:64, h * 512:(h + 1) * 512], in0=pB[0:64, :], in1=bcf[0:64, :]))
                p.wait("tensor", aev)
            woev = [None, None]
            for qsub in range(4):
                for ct in range(2):
                    bk = (qsub * 2 + ct) % 2
                    if woev[bk] is not None:
                        p.wait("tensor", woev[bk])
                    mt = None
                    for h in range(NH):
                        mt = p.pe(lambda t, h=h, qsub=qsub, ct=ct, bk=bk: t.matmul(
                            out=pS[bk][:],
                            lhsT=abuf[0:64, h * 512 + qsub * 128: h * 512 + qsub * 128 + 128],
                            rhs=wos2[0:64, h * 1024 + ct * 512: h * 1024 + ct * 512 + 512],
                            start=(h == 0), stop=(h == NH - 1)), inc=(h == NH - 1))
                    p.wait("vector", mt)
                    xt = x_res[:, (qg * 4 + qsub) * D + ct * 512: (qg * 4 + qsub) * D + ct * 512 + 512]
                    woev[bk] = p.op("vector", lambda v, xt=xt, bk=bk: v.tensor_add(
                        out=xt, in0=xt, in1=pS[bk][:]))
            p.wait("tensor", woev[0]); p.wait("tensor", woev[1])
        p.barrier()
        sub2.close()
        sctx.close()

    p.wait("vector", tok_x)
    p.barrier()
    stages = [stage_attn_a, lambda: stage_ffn(True), stage_attn_b, lambda: stage_ffn(False)]
    for i in range(nstages):
        stages[i]()
        if i in dbg:
            tok = p.dma("sync", dbg[i].rearrange("(tt q) d -> q tt d", q=128),
                        x_res[:].rearrange("q (tt d) -> q tt d", d=D))
            p.wait("sync", tok)
    p.barrier()
    tok = p.dma("sync", x_out.rearrange("(tt q) d -> q tt d", q=128),
                x_res[:].rearrange("q (tt d) -> q tt d", d=D))
    p.wait("sync", tok)
    p.realize()
    ctx.close()
    return nc


def kernel(L0_attn_a_x, L0_attn_a_wq, L0_attn_a_wk, L0_attn_a_wv, L0_attn_a_w1,
           L0_attn_a_w2, L0_attn_a_gn_scale, L0_attn_a_gn_bias, L0_attn_b_Wq,
           L0_attn_b_Wk, L0_attn_b_Wv, L0_attn_b_Wo, L0_ffn_w_gate,
           L0_ffn_w_up, L0_ffn_w_down, _nstages=4, _debug_taps=()):
    x = np.ascontiguousarray(np.asarray(L0_attn_a_x, dtype=np.float32))
    consts = host_constants()
    nc = build(nstages=_nstages, debug_taps=_debug_taps)
    F = lambda a: np.ascontiguousarray(np.asarray(a, dtype=np.float32))
    shared = {
        "wq_a": F(L0_attn_a_wq), "wk_a": F(L0_attn_a_wk), "wv_a": F(L0_attn_a_wv),
        "w1_a": F(L0_attn_a_w1), "w2_a": F(L0_attn_a_w2),
        "gns": F(L0_attn_a_gn_scale), "gnb": F(L0_attn_a_gn_bias),
        "Wq_b": F(L0_attn_b_Wq), "Wk_b": F(L0_attn_b_Wk),
        "Wv_b": F(L0_attn_b_Wv), "Wo_b": F(L0_attn_b_Wo),
        "wg_f": F(L0_ffn_w_gate), "wu_f": F(L0_ffn_w_up), "wd_f": F(L0_ffn_w_down),
        "identc": consts["ident"], "triTc": consts["triT"],
        "qsc": consts["qs"], "knc": consts["kn"], "kpc": consts["kp"],
        "ropeCc": consts["ropeC"], "ropeSc": consts["ropeS"],
    }
    in_maps = [dict(shared, x_in=np.ascontiguousarray(x[c])) for c in range(8)]
    res = run_bass_kernel_spmd(nc, in_maps, core_ids=list(range(8)))
    out = np.stack([np.asarray(r["x_out"], dtype=np.float32) for r in res.results])
    if _debug_taps:
        return out, [np.stack([np.asarray(r[f"dbg{i}"], np.float32) for r in res.results]) for i in _debug_taps]
    return out
